# version 09
# baseline (speedup 1.0000x reference)
import numpy as np
import concourse.bass as bass
import concourse.mybir as mybir
from concourse.bass_utils import run_bass_kernel_spmd

F32 = mybir.dt.float32
BF16 = mybir.dt.bfloat16
AF = mybir.ActivationFunctionType
OP = mybir.AluOpType
AX = mybir.AxisListType.X

NCORE = 8
S = 8192
TOK = 1024
D = 4096
DT = 32
NH = 4
HD = 128
NE = 64
TOPK = 16
RANK = 16
KR = 1024
EPS = 1e-5
EXP_SHIFT = -10.0
SCORE_SCALE = 1.0 / np.sqrt(128.0)
NEG_BIG = -1e30
AGC = 2
AGT = TOK // AGC


class Ctx:
    def __init__(self, nc):
        self.nc = nc
        self.sems = {}
        self.cnt = {}

    def canon(self, name):
        for a, b in (("F2", "FF"), ("F1", "FF"), ("L3", "LL"), ("L2", "LL"),
                     ("L1", "LL")):
            if name.startswith(a):
                name = b + name[len(a):]
                break
        return name

    def sem(self, name):
        name = self.canon(name)
        if name not in self.sems:
            self.sems[name] = self.nc.alloc_semaphore(name)
            self.cnt[name] = 0
        return self.sems[name]

    def inc(self, inst, name, k=1):
        h = self.sem(name)
        name = self.canon(name)
        inst.then_inc(h, k)
        self.cnt[name] += k
        return self.cnt[name]

    def val(self, name):
        self.sem(name)
        return self.cnt[self.canon(name)]

    def wait(self, eng, name, v=None):
        h = self.sem(name)
        if v is None:
            v = self.cnt[self.canon(name)]
        if v > 0:
            eng.wait_ge(h, v)


def build_program(phases=6, p4h=NH, p4q=8, p4sk=(), p1dump=False, p4dump=False):
    nc = bass.Bass("TRN2", target_bir_lowering=False, num_devices=NCORE,
                   detect_race_conditions=False)
    t, v, sc, g, sy = nc.tensor, nc.vector, nc.scalar, nc.gpsimd, nc.sync
    C = Ctx(nc)

    xT_h = nc.dram_tensor("xT", [D, TOK], F32, kind="ExternalInput")
    winT_h = nc.dram_tensor("winT", [D, 768], F32, kind="ExternalInput")
    woutT_h = nc.dram_tensor("woutT", [D, D], F32, kind="ExternalInput")
    AT_h = nc.dram_tensor("AT", [D, KR], F32, kind="ExternalInput")
    Bf_h = nc.dram_tensor("Bf", [KR, D], F32, kind="ExternalInput")
    WrT_h = nc.dram_tensor("WrT", [D, NE], F32, kind="ExternalInput")
    cosT_h = nc.dram_tensor("cosT", [128, S], F32, kind="ExternalInput")
    sinT_h = nc.dram_tensor("sinT", [128, S], F32, kind="ExternalInput")
    Rm_h = nc.dram_tensor("Rm", [128, 128], F32, kind="ExternalInput")
    ident_h = nc.dram_tensor("ident", [128, 128], F32, kind="ExternalInput")
    expc_h = nc.dram_tensor("expc", [NE, KR], F32, kind="ExternalInput")
    tri2_h = nc.dram_tensor("tri2", [128, 1920], F32, kind="ExternalInput")
    outT_h = nc.dram_tensor("outT", [D, TOK], F32, kind="ExternalOutput")

    y2in = [nc.dram_tensor(f"y2in{i}", [D, AGT], BF16) for i in range(AGC)]
    y2out = [nc.dram_tensor(f"y2out{i}", [NCORE * D, AGT], BF16) for i in range(AGC)]
    ctxin = [nc.dram_tensor(f"ctxin{h}", [128, S], BF16) for h in range(NH)]
    ctxout = [nc.dram_tensor(f"ctxout{h}", [NCORE * 128, S], BF16) for h in range(NH)]
    qspill = nc.dram_tensor("qspill", [NH * 128, S], BF16)
    x1spill = nc.dram_tensor("x1spill", [128, DT * TOK], BF16)

    PX = nc.alloc_sbuf_tensor("PX", [128, DT * TOK], BF16)
    PY = nc.alloc_sbuf_tensor("PY", [128, DT * TOK], BF16)
    PU = nc.alloc_sbuf_tensor("PU", [128, 24576], BF16)
    SCR = nc.alloc_sbuf_tensor("SCR", [128, 6144], BF16)
    SCF = nc.alloc_sbuf_tensor("SCF", [128, 1280], F32)
    CONST = nc.alloc_sbuf_tensor("CONST", [128, 2048], BF16)
    onesb = nc.alloc_sbuf_tensor("onesb", [128, 128], BF16)
    onesr = nc.alloc_sbuf_tensor("onesr", [1, 128], F32)
    onescol = nc.alloc_sbuf_tensor("onescol", [128, 1], BF16)
    statr = nc.alloc_sbuf_tensor("statr", [1, 2048], F32)
    cols = nc.alloc_sbuf_tensor("cols", [128, 64], F32)
    zrow = statr[0:1, 0:1024]
    Zf = SCF[:, 256:1280]

    def xt(i, c0=0, w=TOK):
        return PX[:, TOK * i + c0: TOK * i + c0 + w]

    def yt(i, c0=0, w=TOK):
        return PY[:, TOK * i + c0: TOK * i + c0 + w]

    Crope = lambda c0, w: PX[:, 0 * S + c0: 0 * S + c0 + w]
    Srope = lambda c0, w: PX[:, 1 * S + c0: 1 * S + c0 + w]
    KT = lambda c0, w: PX[:, 2 * S + c0: 2 * S + c0 + w]
    Vsb = lambda c0, w: PX[:, 3 * S + c0: 3 * S + c0 + w]

    def y2blk(b, i, c0=0, w=AGT):
        base = b * (DT * AGT)
        return PY[:, base + AGT * i + c0: base + AGT * i + c0 + w]

    def ctxt(e, c0=0, w=TOK):
        return PY[:, TOK * e + c0: TOK * e + c0 + w]

    WrT_sb = lambda i: PU[:, 64 * i: 64 * i + 64]
    ATr = lambda b, i: PU[:, 2048 + b * 4096 + 128 * i: 2048 + b * 4096 + 128 * i + 128]
    Br = lambda b, j: PU[:, 10240 + b * 1024 + 128 * j: 10240 + b * 1024 + 128 * j + 128]
    GE = lambda b, c0, w: PU[:, 12288 + b * 1024 + c0: 12288 + b * 1024 + c0 + w]
    hgT = lambda j, c0, w: PU[:, 14336 + j * 1024 + c0: 14336 + j * 1024 + c0 + w]
    winT_sb = lambda i, f: PU[:, 768 * i + 128 * f: 768 * i + 128 * f + 128]
    qTh = lambda b, c0, w: PU[:, b * S + c0: b * S + c0 + w]
    Pb = lambda b, c0, w: PU[:, 16384 + b * 1024 + c0: 16384 + b * 1024 + c0 + w]
    Zb = lambda c0, w: PU[:, 20480 + c0: 20480 + c0 + w]
    zinv = lambda c0, w: PU[:, 21504 + c0: 21504 + c0 + w]
    ctxst = lambda b, c0, w: PU[:, 22528 + b * 1024 + c0: 22528 + b * 1024 + c0 + w]
    wcol = lambda b, e: PU[:, b * 4096 + 128 * e: b * 4096 + 128 * e + 128]

    mb_sb = lambda c0, w: SCR[:, 2048 + c0: 2048 + c0 + w]
    ib_sb = lambda c0, w: SCR[:, 3072 + c0: 3072 + c0 + w]
    tmp_sb = lambda c0, w: SCR[:, 4096 + c0: 4096 + c0 + w]
    xsq = lambda b, c0, w: SCR[:, 4096 + b * 1024 + c0: 4096 + b * 1024 + c0 + w]
    mask_sb = SCR[:, 0:64]
    G_sb = lambda j: SCR[:, 512 + 64 * j: 512 + 64 * j + 64]
    GT_sb = SCR[0:64, 1024:2048]
    qraw = lambda b: SCR[:, 0 + 512 * b: 512 + 512 * b]
    urop = lambda b: SCR[:, 1024 + 512 * b: 1536 + 512 * b]
    t1rop = lambda b: SCR[:, 2048 + 512 * b: 2560 + 512 * b]
    vrw = SCR[:, 3072:3584]
    qst = lambda b: SCR[:, 4096 + 512 * b: 4608 + 512 * b]
    ln2st = lambda b: SCR[:, 5120 + 512 * b: 5632 + 512 * b]

    s_sb = SCF[:, 0:64]
    swork = SCF[:, 64:128]
    e_sb = SCF[:, 128:192]
    outst = lambda b: SCF[:, 256 + 512 * b: 768 + 512 * b]

    identb = CONST[:, 0:128]
    Rb = CONST[:, 128:256]
    expcb = CONST[0:64, 256:1280]
    tri2b = CONST[:, 0:1920]

    mx = cols[:, 0:8]
    negmx = cols[:, 8:16]
    thr = cols[:, 16:24]
    zc = cols[:, 24:32]
    zic = cols[:, 32:40]
    CST_EXPSHIFT = cols[:, 40:41]
    CST_SCALE = cols[:, 41:42]

    statr_mean = statr[0:1, 0:1024]
    statr_istd = statr[0:1, 1024:2048]
    statr_t = SCF[0:1, 0:1024]
    ones_col = onescol[:, 0:1]

    def vfence(sem_name):
        C.inc(v.tensor_copy(cols[:, 49:50], cols[:, 48:49]), sem_name)

    def sfence(sem_name):
        C.inc(sc.copy(cols[:, 51:52], cols[:, 50:51]), sem_name)

    g.memset(onesb[:, :], 1.0)
    g.memset(onesr[:, :], 1.0)
    g.memset(onescol[:, :], 1.0)
    g.memset(CST_EXPSHIFT, EXP_SHIFT)
    g.memset(CST_SCALE, SCORE_SCALE)
    g.memset(cols[:, 48:52], 0.0)
    C.inc(g.dma_start(out=identb, in_=ident_h[:, :]), "d_cst", 16)
    C.inc(g.dma_start(out=Rb, in_=Rm_h[:, :]), "d_cst", 16)
    C.inc(g.dma_start(out=expcb, in_=expc_h[:, :]), "d_cst", 16)
    for b8 in range(8):
        C.inc(g.dma_start(
            out=PX[:, 4 * TOK * b8: 4 * TOK * (b8 + 1)].rearrange("p (t n) -> p t n", t=4),
            in_=xT_h.ap().rearrange("(t p) n -> p t n", p=128)[:, 4 * b8: 4 * b8 + 4, :],
        ), "d_x", 16)
    C.inc(g.dma_start(out=PU[:, 0:2048].rearrange("p (t n) -> p t n", t=DT),
                      in_=WrT_h.ap().rearrange("(t p) n -> p t n", p=128)), "d_wr", 16)

    def emit_dump_and_stop():
        C.wait(g, "F1vres", C.val("F1vres"))
        C.wait(g, "v_r5", C.val("v_r5"))
        C.wait(g, "L2ln", C.val("L2ln"))
        C.wait(g, "cc_y2", C.val("cc_y2"))
        C.wait(g, "cc_ctx", C.val("cc_ctx"))
        for _hh in range(2):
            for _p in range(2):
                C.wait(g, f"d_ctxh{_hh}p{_p}", C.val(f"d_ctxh{_hh}p{_p}"))
        C.wait(g, "d_qsp0", C.val("d_qsp0"))
        C.wait(g, "d_qsp1", C.val("d_qsp1"))
        C.wait(g, "t_av", C.val("t_av"))
        for b8 in range(8):
            C.inc(g.dma_start(
                out=outT_h.ap().rearrange("(t p) n -> p t n", p=128)[:, 4 * b8:4 * b8 + 4, :],
                in_=PX[:, 4 * TOK * b8:4 * TOK * (b8 + 1)].rearrange("p (t n) -> p t n", t=4)),
                "d_dump", 16)
        C.wait(g, "d_dump", 16 * 8)

    def emit_ln(tag, src_tile, dst_tile, chunked_dma=None):
        v_tag, t_tag, s_tag = f"{tag}v", f"{tag}t", f"{tag}s"
        tb = C.val(t_tag)
        with (
            nc.psum_tensor(f"{tag}psum", [1, 1024], F32) as ps_s,
            nc.psum_tensor(f"{tag}psq", [1, 1024], F32) as ps_q,
        ):
            for i in range(DT):
                if i >= 2:
                    C.wait(v, t_tag, tb + i - 1)
                v.tensor_mul(xsq(i % 2, 0, TOK), src_tile(i), src_tile(i))
                vfence(v_tag)
                for h2 in range(2):
                    t.matmul(ps_s[0:1, 512 * h2:512 * (h2 + 1)], ones_col,
                             src_tile(i, 512 * h2, 512), start=(i == 0), stop=(i == DT - 1))
                C.wait(t, v_tag)
                inst = None
                for h2 in range(2):
                    inst = t.matmul(ps_q[0:1, 512 * h2:512 * (h2 + 1)], ones_col,
                                    xsq(i % 2, 512 * h2, 512), start=(i == 0), stop=(i == DT - 1))
                C.inc(inst, t_tag)
            C.wait(v, t_tag)
            v.tensor_scalar(statr_mean, ps_s[0:1, :], 1.0 / D, None, op0=OP.mult)
            v.tensor_scalar(statr_t, ps_q[0:1, :], 1.0 / D, None, op0=OP.mult)
            v.tensor_mul(statr_istd, statr_mean, statr_mean)
            v.tensor_sub(statr_t, statr_t, statr_istd)
            v.tensor_scalar(statr_t, statr_t, EPS, None, op0=OP.add)
            vfence(v_tag)
            C.wait(sc, v_tag)
            sc.activation(statr_t, statr_t, AF.Sqrt)
            sfence(s_tag)
            C.wait(v, s_tag)
            v.reciprocal(statr_istd, statr_t)
            vfence(v_tag)
        with (
            nc.psum_tensor(f"{tag}pmb", [128, 1024], F32) as ps_mb,
            nc.psum_tensor(f"{tag}pib", [128, 1024], F32) as ps_ib,
        ):
            C.wait(t, v_tag)
            inst = None
            for h2 in range(2):
                t.matmul(ps_mb[:, 512 * h2:512 * (h2 + 1)], onesr[:, :],
                         statr_mean[:, 512 * h2:512 * (h2 + 1)], start=True, stop=True)
                inst = t.matmul(ps_ib[:, 512 * h2:512 * (h2 + 1)], onesr[:, :],
                                statr_istd[:, 512 * h2:512 * (h2 + 1)], start=True, stop=True)
            C.inc(inst, t_tag)
            C.wait(sc, t_tag)
            sc.copy(mb_sb(0, TOK), ps_mb[:, :])
            sc.copy(ib_sb(0, TOK), ps_ib[:, :])
            sfence(s_tag)
            C.wait(v, s_tag)
            if chunked_dma is None:
                for i in range(DT):
                    v.tensor_sub(tmp_sb(0, TOK), src_tile(i), mb_sb(0, TOK))
                    v.tensor_mul(dst_tile(i), tmp_sb(0, TOK), ib_sb(0, TOK))
                    vfence(f"{tag}ln")
            else:
                n = 0
                dma_log = []
                for ct in range(AGC):
                    for i in range(DT):
                        b = n % 2
                        if n >= 2:
                            pname, pcnt = dma_log[n - 2]
                            C.wait(v, pname, pcnt)
                        v.tensor_sub(tmp_sb(0, AGT), src_tile(i, AGT * ct, AGT),
                                     mb_sb(AGT * ct, AGT))
                        v.tensor_mul(ln2st(b), tmp_sb(0, AGT), ib_sb(AGT * ct, AGT))
                        vfence(f"{tag}ln")
                        C.wait(sy, f"{tag}ln")
                        sname = f"{tag}dma{ct}p{b}"
                        cnt = C.inc(chunked_dma(i, ct, ln2st(b)), sname, 16)
                        dma_log.append((sname, cnt))
                        n += 1

    def emit_ffn(tag, ytile, res_out_dma):
        tsc, vsc, vtk, se, vG, tgt, sgt = (f"{tag}tsc", f"{tag}vsc", f"{tag}vtk",
                                           f"{tag}se", f"{tag}vG", f"{tag}tgt",
                                           f"{tag}sgt")
        tge, sge, th, vhg, to, vres = (f"{tag}tge", f"{tag}sge", f"{tag}th",
                                       f"{tag}vhg", f"{tag}to", f"{tag}vres")
        vs_ = f"{tag}vs"
        dA, dB = f"{tag}dA", f"{tag}dB"
        se_b, vG_b, tgt_b, sgt_b = C.val(se), C.val(vG), C.val(tgt), C.val(sgt)
        vsc_b, tge_b, sge_b, th_b = C.val(vsc), C.val(tge), C.val(sge), C.val(th)
        vhg_b, to_b, dA_b, dB_b = C.val(vhg), C.val(to), C.val(dA), C.val(dB)
        with (
            nc.psum_tensor(f"{tag}psc", [128, 1024], F32) as ps_sc,
            nc.psum_tensor(f"{tag}pgt", [64, 128], BF16) as ps_gt,
        ):
            C.wait(t, "d_wr")
            for j in range(8):
                if j >= 2:
                    C.wait(t, vsc, vsc_b + j - 1)
                inst = None
                for i in range(DT):
                    inst = t.matmul(ps_sc[:, 512 * (j % 2):512 * (j % 2) + 64],
                                    ytile(i, 128 * j, 128), WrT_sb(i),
                                    start=(i == 0), stop=(i == DT - 1))
                C.inc(inst, tsc)
                C.wait(v, tsc)
                if j >= 1:
                    C.wait(v, se, se_b + 2 * j)
                C.inc(v.tensor_copy(s_sb, ps_sc[:, 512 * (j % 2):512 * (j % 2) + 64]),
                      vsc)
                v.tensor_copy(swork, s_sb)
                for r_ in range(TOPK):
                    dstm = mx[:, j:j + 1] if r_ == 0 else thr[:, j:j + 1]
                    C.inc(v.reduce_max(dstm, swork, axis=AX), vs_)
                    C.wait(v, vs_)
                    v.tensor_scalar(e_sb, swork, dstm, NEG_BIG,
                                    op0=OP.is_equal, op1=OP.mult)
                    v.tensor_add(swork, swork, e_sb)
                v.tensor_scalar(negmx[:, j:j + 1], mx[:, j:j + 1], -1.0, None,
                                op0=OP.mult)
                C.inc(v.tensor_scalar(mask_sb, s_sb, thr[:, j:j + 1], None,
                                      op0=OP.is_ge), vtk)
                C.wait(sc, vtk)
                C.inc(sc.activation(e_sb, s_sb, AF.Exp, bias=negmx[:, j:j + 1]), se)
                sfence(se)
                C.wait(v, se)
                v.tensor_mul(e_sb, e_sb, mask_sb)
                C.inc(v.reduce_sum(zc[:, j:j + 1], e_sb, axis=AX), vs_)
                C.wait(v, vs_)
                C.inc(v.reciprocal(zic[:, j:j + 1], zc[:, j:j + 1]), vs_)
                C.wait(v, vs_)
                v.tensor_scalar(G_sb(j), e_sb, zic[:, j:j + 1], None,
                                op0=OP.mult)
                vfence(vG)
                C.wait(t, vG, vG_b + j + 1)
                if j >= 1:
                    C.wait(t, sgt, sgt_b + j)
                C.inc(t.transpose(ps_gt[:, :], G_sb(j), identb), tgt)
                C.wait(sc, tgt, tgt_b + j + 1)
                C.inc(sc.copy(GT_sb[:, 128 * j:128 * j + 128], ps_gt[:, :]), sgt)
                if j == 7:
                    sfence(sgt)
        for q_ in range(4):
            C.inc(g.dma_start(
                out=PU[:, 2048 + 1024 * q_:2048 + 1024 * (q_ + 1)]
                    .rearrange("p (t n) -> p t n", t=8),
                in_=AT_h.ap()[:, 0:128]
                    .rearrange("(t p) n -> p t n", p=128)[:, 8 * q_:8 * q_ + 8, :]),
                f"{dA}p0", 16)
        with (
            nc.psum_tensor(f"{tag}ph", [128, 1024], F32) as ps_h,
            nc.psum_tensor(f"{tag}pge", [128, 1024], F32) as ps_ge,
        ):
            C.wait(t, sgt, sgt_b + 9)
            for j in range(8):
                if j + 1 < 8:
                    if j + 1 >= 2:
                        C.wait(g, th, th_b + 2 * j)
                    for q_ in range(4):
                        C.inc(g.dma_start(
                            out=PU[:, 2048 + ((j + 1) % 2) * 4096 + 1024 * q_:
                                   2048 + ((j + 1) % 2) * 4096 + 1024 * (q_ + 1)]
                                .rearrange("p (t n) -> p t n", t=8),
                            in_=AT_h.ap()[:, 128 * (j + 1):128 * (j + 2)]
                                .rearrange("(t p) n -> p t n", p=128)[:, 8 * q_:8 * q_ + 8, :]),
                            f"{dA}p{(j + 1) % 2}", 16)
                if j >= 1:
                    C.wait(t, sge, sge_b + 2 * j)
                if j >= 2:
                    C.wait(t, vhg, vhg_b + 2 * (j - 1))
                inst = None
                for h2 in range(2):
                    inst = t.matmul(ps_ge[:, 512 * h2:512 * (h2 + 1)],
                                    expcb[:, 128 * j:128 * j + 128],
                                    GT_sb[:, 512 * h2:512 * (h2 + 1)],
                                    start=True, stop=True)
                C.inc(inst, tge)
                C.wait(sc, tge, tge_b + j + 1)
                C.inc(sc.copy(GE(j % 2, 0, TOK), ps_ge[:, :]), sge)
                sfence(sge)
                C.wait(t, f"{dA}p{j % 2}", C.val(f"{dA}p{j % 2}"))
                if j >= 1:
                    C.wait(t, vhg, vhg_b + 2 * j)
                for t2 in range(2):
                    inst = None
                    for i in range(DT):
                        inst = t.matmul(ps_h[:, 512 * t2:512 * (t2 + 1)],
                                        ATr(j % 2, i), ytile(i, 512 * t2, 512),
                                        start=(i == 0), stop=(i == DT - 1))
                    C.inc(inst, th)
                    C.wait(v, th)
                    C.wait(v, sge, sge_b + 2 * (j + 1))
                    C.inc(v.tensor_mul(hgT(j, 512 * t2, 512),
                                       ps_h[:, 512 * t2:512 * (t2 + 1)],
                                       GE(j % 2, 512 * t2, 512)), vhg)
                    if j == 7 and t2 == 1:
                        vfence(vhg)
        C.inc(g.dma_start(
            out=PU[:, 10240:10240 + 1024].rearrange("p (t n) -> p t n", t=8),
            in_=Bf_h.ap()[:, 0:128].rearrange("(t p) n -> p t n", p=128)), f"{dB}p0", 16)
        vres_b = C.val(vres)
        with (
            nc.psum_tensor(f"{tag}po0", [128, 1024], F32) as ps_o0,
            nc.psum_tensor(f"{tag}po1", [128, 1024], F32) as ps_o1,
        ):
            C.wait(t, vhg, vhg_b + 17)
            for i in range(DT):
                ps_o = ps_o0 if i % 2 == 0 else ps_o1
                if i + 1 < DT:
                    if i + 1 >= 2:
                        C.wait(g, to, to_b + 2 * i)
                    C.inc(g.dma_start(
                        out=PU[:, 10240 + ((i + 1) % 2) * 1024:
                               10240 + ((i + 1) % 2) * 1024 + 1024]
                            .rearrange("p (t n) -> p t n", t=8),
                        in_=Bf_h.ap()[:, 128 * (i + 1):128 * (i + 2)]
                            .rearrange("(t p) n -> p t n", p=128)), f"{dB}p{(i + 1) % 2}", 16)
                C.wait(t, f"{dB}p{i % 2}", C.val(f"{dB}p{i % 2}"))
                if i >= 2:
                    C.wait(t, vres, vres_b + 2 * (i - 1))
                for t2 in range(2):
                    inst = None
                    for j in range(8):
                        inst = t.matmul(ps_o[:, 512 * t2:512 * (t2 + 1)],
                                        Br(i % 2, j), hgT(j, 512 * t2, 512),
                                        start=(j == 0), stop=(j == 7))
                    C.inc(inst, to)
                    C.wait(v, to)
                    if res_out_dma is None:
                        v.tensor_add(xt(i, 512 * t2, 512), xt(i, 512 * t2, 512),
                                     ps_o[:, 512 * t2:512 * (t2 + 1)])
                        vfence(vres)
                    else:
                        n = 2 * i + t2
                        if n >= 2:
                            C.wait(v, f"{tag}doutp{n % 2}", C.val(f"{tag}doutp{n % 2}"))
                        v.tensor_add(outst(n % 2),
                                     ps_o[:, 512 * t2:512 * (t2 + 1)],
                                     xt(i, 512 * t2, 512))
                        vfence(vres)
                        C.wait(sy, vres)
                        C.inc(res_out_dma(i, t2, outst(n % 2)), f"{tag}doutp{n % 2}", 16)

    C.wait(v, "d_x", 16 * 8)
    C.wait(t, "d_x", 16 * 8)
    emit_ln("L1", lambda i, c0=0, w=TOK: xt(i, c0, w), lambda i: yt(i))

    C.wait(t, "L1ln", C.val("L1ln"))
    C.wait(t, "d_cst", 48)
    emit_ffn("F1", lambda i, c0=0, w=TOK: yt(i, c0, w), None)

    if phases <= 1:
        if p1dump:
            dbgY = nc.dram_tensor("dbgY", [128, DT * TOK], F32, kind="ExternalOutput")
            dbgH = nc.dram_tensor("dbgH", [128, 8192], F32, kind="ExternalOutput")
            dbgG = nc.dram_tensor("dbgG", [64, 1024], F32, kind="ExternalOutput")
            C.wait(g, "F1vres", C.val("F1vres"))
            for b8 in range(8):
                C.inc(g.dma_start(out=dbgY.ap()[:, 4096 * b8:4096 * (b8 + 1)],
                                  in_=PY[:, 4096 * b8:4096 * (b8 + 1)]), "d_dbg", 16)
            for b2 in range(2):
                C.inc(g.dma_start(out=dbgH.ap()[:, 4096 * b2:4096 * (b2 + 1)],
                                  in_=PU[:, 14336 + 4096 * b2:14336 + 4096 * (b2 + 1)]),
                      "d_dbg", 16)
            C.inc(g.dma_start(out=dbgG.ap()[:, :], in_=GT_sb), "d_dbg", 16)
            for b8 in range(8):
                C.inc(g.dma_start(out=dbgY2.ap()[4096 * b8:4096 * (b8 + 1), :],
                                  in_=y2out[1].ap()[4096 * b8:4096 * (b8 + 1), :]),
                      "d_dbg", 16)
            C.inc(g.dma_start(out=dbgY2i.ap()[:, :], in_=y2in[1].ap()[:, :]),
                  "d_dbg", 16)
            C.wait(g, "d_dbg", C.val("d_dbg"))
        emit_dump_and_stop()
        return nc
    C.wait(g, "F1to", C.val("F1to"))
    C.wait(g, "F1vhg", C.val("F1vhg"))
    for b8 in range(8):
        C.inc(g.dma_start(
            out=PU[:, 3072 * b8: 3072 * (b8 + 1)].rearrange("p (t n) -> p t n", t=4),
            in_=winT_h.ap().rearrange("(t p) n -> p t n", p=128)[:, 4 * b8:4 * b8 + 4, :]),
            "d_win", 16)

    def ln2_dma(i, ct, stage):
        return sy.dma_start(out=y2in[ct].ap()[128 * i:128 * (i + 1), :], in_=stage)

    C.wait(v, "F1vres", C.val("F1vres"))
    C.wait(t, "F1vres", C.val("F1vres"))
    emit_ln("L2", lambda i, c0=0, w=TOK: xt(i, c0, w), None, chunked_dma=ln2_dma)

    for ct in range(AGC):
        C.wait(g, f"L2dma{ct}p0", C.val(f"L2dma{ct}p0"))
        C.wait(g, f"L2dma{ct}p1", C.val(f"L2dma{ct}p1"))
        C.inc(g.collective_compute(
            "AllGather", OP.bypass, replica_groups=[list(range(NCORE))],
            ins=[y2in[ct].ap().opt()], outs=[y2out[ct].ap().opt()]), "cc_y2")

    C.wait(g, "L2ln", C.val("L2ln"))
    for b8 in range(8):
        C.inc(g.dma_start(out=x1spill.ap()[:, 4096 * b8:4096 * (b8 + 1)],
                          in_=PX[:, 4096 * b8:4096 * (b8 + 1)]), "d_x1s", 16)
    C.wait(g, "d_x1s", 16 * 8)
    for b4 in range(4):
        C.inc(g.dma_start(out=PX[:, 0 * S + 2048 * b4: 0 * S + 2048 * (b4 + 1)],
                          in_=cosT_h.ap()[:, 2048 * b4:2048 * (b4 + 1)]), "d_cs", 16)
        C.inc(g.dma_start(out=PX[:, 1 * S + 2048 * b4: 1 * S + 2048 * (b4 + 1)],
                          in_=sinT_h.ap()[:, 2048 * b4:2048 * (b4 + 1)]), "d_cs", 16)

    if phases <= 2:
        emit_dump_and_stop()
        return nc
    with (
        nc.psum_tensor("p3a", [128, 512], F32) as p3a,
        nc.psum_tensor("p3b", [128, 512], F32) as p3b,
        nc.psum_tensor("p3r", [128, 512], F32) as p3r,
        nc.psum_tensor("p3t0", [128, 128], BF16) as p3t0,
        nc.psum_tensor("p3t1", [128, 128], BF16) as p3t1,
    ):
        p3ab = [p3a, p3b]
        p3ts = [p3t0, p3t1]
        C.wait(sc, "L2ln", C.val("L2ln"))
        C.wait(t, "d_win", 16 * 8)
        C.wait(v, "d_cs", 16 * 8)
        nblk = 0
        for ct in range(AGC):
            C.wait(sy, "cc_y2", ct + 1)
            for r in range(NCORE):
                b = nblk % 2
                if nblk >= 2:
                    C.wait(sy, "t_qkv", 6 * (nblk - 1))
                for q_ in range(4):
                    C.inc(sy.dma_start(
                        out=PY[:, b * (DT * AGT) + 8 * AGT * q_:
                               b * (DT * AGT) + 8 * AGT * (q_ + 1)]
                            .rearrange("p (t n) -> p t n", t=8),
                        in_=y2out[ct].ap()[D * r:D * (r + 1), :]
                            .rearrange("(t p) n -> p t n", p=128)[:, 8 * q_:8 * q_ + 8, :]),
                        f"d_y2bp{b}", 16)
                C.wait(t, f"d_y2bp{b}", C.val(f"d_y2bp{b}"))
                tok0 = r * TOK + ct * AGT
                for f in range(6):
                    pf = p3ab[f % 2]
                    C.wait(t, "s_qe", max(0, C.val("s_qe") - 1))
                    inst = None
                    for i in range(DT):
                        inst = t.matmul(pf[:, :], winT_sb(i, f), y2blk(b, i),
                                        start=(i == 0), stop=(i == DT - 1))
                    C.inc(inst, "t_qkv")
                    C.wait(sc, "t_qkv")
                    if f < 5:
                        C.wait(sc, "v_qf", C.val("v_qf"))
                        C.inc(sc.copy(qraw(f % 2), pf[:, :]), "s_qe")
                        sfence("s_qf")
                        C.wait(v, "s_qf", C.val("s_qf"))
                        C.wait(v, "t_rot", C.val("t_rot"))
                        v.tensor_mul(urop(f % 2), qraw(f % 2), Srope(tok0, AGT))
                        vfence("v_u")
                        C.wait(t, "v_u")
                        C.wait(t, "v_qf", C.val("v_qf"))
                        C.inc(t.matmul(p3r[:, :], Rb, urop(f % 2),
                                       start=True, stop=True), "t_rot")
                        v.tensor_mul(t1rop(f % 2), qraw(f % 2), Crope(tok0, AGT))
                        C.wait(v, "t_rot")
                        if f < 4:
                            C.wait(v, f"d_qsp{f % 2}", C.val(f"d_qsp{f % 2}"))
                            dst = qst(f % 2)
                        else:
                            dst = KT(tok0, AGT)
                        v.tensor_add(dst, t1rop(f % 2), p3r[:, :])
                        vfence("v_qf")
                        if f < 4:
                            C.wait(sy, "v_qf")
                            C.inc(sy.dma_start(
                                out=qspill.ap()[128 * f:128 * (f + 1), tok0:tok0 + AGT],
                                in_=qst(f % 2)), f"d_qsp{f % 2}", 16)
                    else:
                        C.wait(sc, "t_vt", C.val("t_vt"))
                        C.inc(sc.copy(vrw, pf[:, :]), "s_qe")
                        sfence("s_qf")
                        C.wait(t, "s_qf", C.val("s_qf"))
                        for q4 in range(4):
                            if C.val("t_vt") >= 2:
                                C.wait(t, "s_vt", C.val("t_vt") - 1)
                            C.inc(t.transpose(p3ts[q4 % 2][:, :],
                                              vrw[:, 128 * q4:128 * q4 + 128],
                                              identb), "t_vt")
                            C.wait(sc, "t_vt")
                            C.inc(sc.copy(Vsb(tok0 + 128 * q4, 128),
                                          p3ts[q4 % 2][:, :]), "s_vt")
                nblk += 1
        sfence("s_vt")
        C.wait(g, "t_rot", C.val("t_rot"))
        C.wait(g, "t_vt", C.val("t_vt"))
    C.inc(g.dma_start(out=tri2b, in_=tri2_h[:, :]), "d_tri", 16)

    if phases <= 3:
        emit_dump_and_stop()
        return nc
    with (
        nc.psum_tensor("p4s0", [128, 1024], F32) as p4s0,
        nc.psum_tensor("p4s1", [128, 1024], F32) as p4s1,
        nc.psum_tensor("p4c", [128, 1024], F32) as p4c,
        nc.psum_tensor("p4z", [1, 1024], F32) as p4z,
    ):
        p4s = [p4s0, p4s1]
        C.wait(v, "d_tri", 16)
        C.wait(t, "v_qf", C.val("v_qf"))
        C.wait(t, "s_vt", C.val("s_vt"))
        C.wait(sy, "t_qkv", C.val("t_qkv"))
        for hh in range(2):
            for b4 in range(4):
                C.inc(sy.dma_start(out=qTh(hh % 2, 2048 * b4, 2048),
                                   in_=qspill.ap()[128 * hh:128 * (hh + 1),
                                                   2048 * b4:2048 * (b4 + 1)]),
                      f"d_qhp{hh % 2}", 16)
        ctx_dma_log = []
        for hh in range(p4h):
            bq = hh % 2
            C.wait(t, f"d_qhp{hh % 2}", 64 * (hh // 2 + 1))
            for qc in range(p4q):
                KB = 8 * (qc + 1)
                se_b = C.val("s_exp")
                vz_b = C.val("v_z4")
                av_b = C.val("t_av")
                zf_fresh = True
                zb_fresh = True

                def emit_vec(kbv):
                    nonlocal zf_fresh, zb_fresh
                    C.wait(v, "s_exp", se_b + kbv + 2)
                    if kbv == 0:
                        C.wait(v, "t_z", C.val("t_z"))
                    if kbv >= 8 * qc:
                        qlo = 128 * (kbv - 8 * qc)
                        v.tensor_mul(Pb(kbv % 4, 0, TOK), Pb(kbv % 4, 0, TOK),
                                     tri2b[:, 896 - qlo: 1920 - qlo])
                    if zb_fresh:
                        inst = v.tensor_copy(Zb(0, TOK), Pb(kbv % 4, 0, TOK))
                        zb_fresh = False
                    else:
                        inst = v.tensor_add(Zb(0, TOK), Zb(0, TOK), Pb(kbv % 4, 0, TOK))
                    if kbv % 8 == 7 or kbv == KB - 1:
                        if zf_fresh:
                            inst = v.tensor_copy(Zf[:, :], Zb(0, TOK))
                            zf_fresh = False
                        else:
                            inst = v.tensor_add(Zf[:, :], Zf[:, :], Zb(0, TOK))
                        zb_fresh = True
                    C.inc(inst, "v_z4")

                def emit_av(kbv, last):
                    C.wait(t, "v_z4", vz_b + kbv + 1)
                    inst = None
                    for h2 in range(2):
                        inst = t.matmul(p4c[:, 512 * h2:512 * (h2 + 1)],
                                        Vsb(128 * kbv, 128),
                                        Pb(kbv % 4, 512 * h2, 512),
                                        start=(kbv == 0), stop=last)
                    C.inc(inst, "t_av")

                def emit_score_exp(kb):
                    pb = kb % 2
                    pp = kb % 4
                    if kb == 0:
                        C.wait(t, "v_zi", C.val("v_zi"))
                        C.wait(t, "v_ctx", C.val("v_ctx"))
                    if kb >= 1:
                        C.wait(t, "s_exp", se_b + max(0, kb - 1))
                    inst = None
                    for h2 in range(2):
                        inst = t.matmul(p4s[pb][:, 512 * h2:512 * (h2 + 1)],
                                        KT(128 * kb, 128),
                                        qTh(bq, 1024 * qc + 512 * h2, 512),
                                        start=True, stop=True)
                    C.inc(inst, "t_sc4")
                    C.wait(sc, "t_sc4")
                    if kb >= 4:
                        C.wait(sc, "t_av", av_b + kb - 3)
                        C.wait(sc, "v_z4", vz_b + kb - 3)
                    C.inc(sc.activation(Pb(pp, 0, TOK), p4s[pb][:, :], AF.Exp,
                                        bias=CST_EXPSHIFT, scale=CST_SCALE), "s_exp")

                for kb2 in range(0, KB, 2):
                    emit_score_exp(kb2)
                    emit_score_exp(kb2 + 1)
                    if kb2 >= 2:
                        emit_vec(kb2 - 2)
                        emit_av(kb2 - 2, last=False)
                        emit_vec(kb2 - 1)
                        emit_av(kb2 - 1, last=False)
                sfence("s_exp")
                emit_vec(KB - 2)
                emit_av(KB - 2, last=False)
                emit_vec(KB - 1)
                vfence("v_z4")
                emit_av(KB - 1, last=True)
                C.wait(sc, "v_z4", C.val("v_z4"))
                C.wait(sc, "t_z", C.val("t_z"))
                sc.copy(Zb(0, TOK), Zf[:, :])
                sfence("s_z16")
                C.wait(t, "s_z16", C.val("s_z16"))
                C.wait(t, "s_exp", C.val("s_exp"))
                C.wait(t, "s_zr", C.val("s_zr"))
                inst = None
                for h2 in range(2):
                    inst = t.matmul(p4z[0:1, 512 * h2:512 * (h2 + 1)], ones_col,
                                    Zb(512 * h2, 512), start=True, stop=True)
                C.inc(inst, "t_z")
                C.wait(sc, "t_z", C.val("t_z"))
                C.wait(sc, "t_zb", C.val("t_zb"))
                sc.copy(zrow, p4z[0:1, :])
                sfence("s_zr")
                C.wait(v, "s_zr", C.val("s_zr"))
                v.reciprocal(statr[0:1, 1024:2048], zrow)
                vfence("v_zr")
                C.wait(t, "v_zr", C.val("v_zr"))
                C.wait(t, "v_zi", C.val("v_zi"))
                inst = None
                for h2 in range(2):
                    inst = t.matmul(p4s0[:, 512 * h2:512 * (h2 + 1)], onesr[:, :],
                                    statr[0:1, 1024 + 512 * h2:1024 + 512 * (h2 + 1)],
                                    start=True, stop=True)
                C.inc(inst, "t_zb")
                C.wait(sc, "t_zb", C.val("t_zb"))
                sc.copy(zinv(0, TOK), p4s0[:, :])
                sfence("v_zi")
                nchunk = hh * 8 + qc
                cb = nchunk % 2
                if nchunk >= 2:
                    pname, pcnt = ctx_dma_log[nchunk - 2]
                    C.wait(v, pname, pcnt)
                C.wait(v, "t_av", C.val("t_av"))
                C.wait(v, "v_zi", C.val("v_zi"))
                for h2 in range(2):
                    v.tensor_mul(ctxst(cb, 512 * h2, 512),
                                 p4c[:, 512 * h2:512 * (h2 + 1)],
                                 zinv(512 * h2, 512))
                vfence("v_ctx")
                C.wait(sy, "v_ctx")
                sname = f"d_ctxh{hh % 2}p{cb}"
                cnt = C.inc(sy.dma_start(out=ctxin[hh].ap()[:, 1024 * qc:1024 * (qc + 1)],
                                         in_=ctxst(cb, 0, TOK)), sname, 16)
                ctx_dma_log.append((sname, cnt))
            if hh + 2 < NH:
                C.wait(sy, "t_sc4", C.val("t_sc4"))
                for b4 in range(4):
                    C.inc(sy.dma_start(out=qTh((hh + 2) % 2, 2048 * b4, 2048),
                                       in_=qspill.ap()[128 * (hh + 2):128 * (hh + 3),
                                                       2048 * b4:2048 * (b4 + 1)]),
                          f"d_qhp{hh % 2}", 16)
            C.wait(g, f"d_ctxh{hh % 2}p0", C.val(f"d_ctxh{hh % 2}p0"))
            C.wait(g, f"d_ctxh{hh % 2}p1", C.val(f"d_ctxh{hh % 2}p1"))
            C.inc(g.collective_compute(
                "AllGather", OP.bypass, replica_groups=[list(range(NCORE))],
                ins=[ctxin[hh].ap().opt()], outs=[ctxout[hh].ap().opt()]), "cc_ctx")

    if phases <= 4:
        if p4dump:
            dbgC = nc.dram_tensor("dbgC", [128, S], F32, kind="ExternalOutput")
            dbgK = nc.dram_tensor("dbgK", [128, S], F32, kind="ExternalOutput")
            dbgV = nc.dram_tensor("dbgV", [128, S], F32, kind="ExternalOutput")
            dbgQ = nc.dram_tensor("dbgQ", [128, S], F32, kind="ExternalOutput")
            dbgY2 = nc.dram_tensor("dbgY2", [NCORE * D, AGT], F32, kind="ExternalOutput")
            dbgY2i = nc.dram_tensor("dbgY2i", [D, AGT], F32, kind="ExternalOutput")
            C.wait(g, "cc_ctx", 1)
            for b4 in range(4):
                C.inc(g.dma_start(out=dbgC.ap()[:, 2048 * b4:2048 * (b4 + 1)],
                                  in_=ctxout[0].ap()[0:128, 2048 * b4:2048 * (b4 + 1)]),
                      "d_dbg", 16)
                C.inc(g.dma_start(out=dbgK.ap()[:, 2048 * b4:2048 * (b4 + 1)],
                                  in_=KT(2048 * b4, 2048)), "d_dbg", 16)
                C.inc(g.dma_start(out=dbgV.ap()[:, 2048 * b4:2048 * (b4 + 1)],
                                  in_=Vsb(2048 * b4, 2048)), "d_dbg", 16)
                C.inc(g.dma_start(out=dbgQ.ap()[:, 2048 * b4:2048 * (b4 + 1)],
                                  in_=qspill.ap()[0:128, 2048 * b4:2048 * (b4 + 1)]),
                      "d_dbg", 16)
            for b8 in range(8):
                C.inc(g.dma_start(out=dbgY2.ap()[4096 * b8:4096 * (b8 + 1), :],
                                  in_=y2out[1].ap()[4096 * b8:4096 * (b8 + 1), :]),
                      "d_dbg", 16)
            C.inc(g.dma_start(out=dbgY2i.ap()[:, :], in_=y2in[1].ap()[:, :]),
                  "d_dbg", 16)
            C.wait(g, "d_dbg", C.val("d_dbg"))
        emit_dump_and_stop()
        return nc
    C.wait(g, "t_av", C.val("t_av"))
    C.wait(g, "t_sc4", C.val("t_sc4"))
    for b8 in range(8):
        C.inc(g.dma_start(out=PX[:, 4096 * b8:4096 * (b8 + 1)],
                          in_=x1spill.ap()[:, 4096 * b8:4096 * (b8 + 1)]), "d_x1r", 16)
    with g.register("ctxoff") as roff:
        pid = g.partition_id()
        for hh in range(NH):
            C.wait(g, "cc_ctx", hh + 1)
            g.reg_alu(roff, pid, TOK, OP.mult)
            src = bass.AP(ctxout[hh], roff, [[S, 128], [S * 128, NCORE], [1, TOK]])
            dst = bass.AP(PY, TOK * hh, [[DT * TOK, 128], [4 * TOK, NCORE], [1, TOK]])
            C.inc(g.dma_start(out=dst, in_=src), "d_ctxr", 16)
    C.wait(g, "t_qkv", C.val("t_qkv"))
    for q_ in range(4):
        C.inc(g.dma_start(
            out=PU[:, 1024 * q_:1024 * (q_ + 1)].rearrange("p (t n) -> p t n", t=8),
            in_=woutT_h.ap()[:, 0:128]
                .rearrange("(t p) n -> p t n", p=128)[:, 8 * q_:8 * q_ + 8, :]), "d_wop0", 16)
    r5_b = C.val("v_r5")
    with (
        nc.psum_tensor("p5o0", [128, 1024], F32) as ps_o50,
        nc.psum_tensor("p5o1", [128, 1024], F32) as ps_o51,
    ):
        C.wait(t, "d_ctxr", 16 * 4)
        C.wait(t, "d_x1r", 16 * 8)
        for fo in range(DT):
            ps_o5 = ps_o50 if fo % 2 == 0 else ps_o51
            if fo + 1 < DT:
                if fo + 1 >= 2:
                    C.wait(g, "t_o5", 2 * fo)
                for q_ in range(4):
                    C.inc(g.dma_start(
                        out=PU[:, ((fo + 1) % 2) * 4096 + 1024 * q_:
                               ((fo + 1) % 2) * 4096 + 1024 * (q_ + 1)]
                            .rearrange("p (t n) -> p t n", t=8),
                        in_=woutT_h.ap()[:, 128 * (fo + 1):128 * (fo + 2)]
                            .rearrange("(t p) n -> p t n", p=128)[:, 8 * q_:8 * q_ + 8, :]),
                        f"d_wop{(fo + 1) % 2}", 16)
            C.wait(t, f"d_wop{fo % 2}", C.val(f"d_wop{fo % 2}"))
            if fo >= 2:
                C.wait(t, "v_r5", r5_b + 2 * (fo - 1))
            for t2 in range(2):
                inst = None
                for e in range(DT):
                    inst = t.matmul(ps_o5[:, 512 * t2:512 * (t2 + 1)],
                                    wcol(fo % 2, e), ctxt(e, 512 * t2, 512),
                                    start=(e == 0), stop=(e == DT - 1))
                C.inc(inst, "t_o5")
                C.wait(v, "t_o5")
                v.tensor_add(xt(fo, 512 * t2, 512), xt(fo, 512 * t2, 512),
                             ps_o5[:, 512 * t2:512 * (t2 + 1)])
                vfence("v_r5")
    C.wait(g, "t_o5", C.val("t_o5"))
    C.inc(g.dma_start(out=identb, in_=ident_h[:, :]), "d_cst", 16)
    C.inc(g.dma_start(out=expcb, in_=expc_h[:, :]), "d_cst", 16)
    C.inc(g.dma_start(out=PU[:, 0:2048].rearrange("p (t n) -> p t n", t=DT),
                      in_=WrT_h.ap().rearrange("(t p) n -> p t n", p=128)), "d_wr", 16)
    C.wait(v, "d_cst", C.val("d_cst"))
    C.wait(t, "d_cst", C.val("d_cst"))
    C.wait(t, "v_r5", C.val("v_r5"))
    C.wait(v, "t_o5", C.val("t_o5"))
    emit_ln("L3", lambda i, c0=0, w=TOK: xt(i, c0, w), lambda i: yt(i))

    if phases <= 5:
        emit_dump_and_stop()
        return nc
    def out_dma(i, t2, stage):
        return sy.dma_start(out=outT_h.ap()[128 * i:128 * (i + 1),
                                            512 * t2:512 * (t2 + 1)], in_=stage)

    C.wait(t, "L3ln", C.val("L3ln"))
    emit_ffn("F2", lambda i, c0=0, w=TOK: yt(i, c0, w), out_dma)

    C.wait(sy, "F2doutp0", C.val("F2doutp0"))
    C.wait(sy, "F2doutp1", C.val("F2doutp1"))
    return nc


def kernel(L0_attn_x, L0_attn_w_in, L0_attn_w_out, L0_ffn_A, L0_ffn_B, L0_ffn_Wr):
    x = np.asarray(L0_attn_x)
    w_in = np.asarray(L0_attn_w_in)
    w_out = np.asarray(L0_attn_w_out)
    A = np.asarray(L0_ffn_A)
    B = np.asarray(L0_ffn_B)
    Wr = np.asarray(L0_ffn_Wr)

    x2d = x.reshape(S, D)
    woutT = np.ascontiguousarray(w_out.T)
    AT = np.ascontiguousarray(A.reshape(KR, D).T)
    Bf = np.ascontiguousarray(B.transpose(0, 2, 1).reshape(KR, D))
    WrT = np.ascontiguousarray(Wr.T)

    inv_freq = (1.0 / (10000.0 ** (np.arange(0, HD, 2, dtype=np.float32)
                                   / np.float32(HD)))).astype(np.float32)
    ang = np.arange(S, dtype=np.float32)[:, None] * inv_freq[None, :]
    cosv = np.cos(ang).astype(np.float32)
    sinv = np.sin(ang).astype(np.float32)
    cosT = np.ascontiguousarray(np.concatenate([cosv.T, cosv.T], axis=0))
    sinT = np.ascontiguousarray(np.concatenate([sinv.T, sinv.T], axis=0))

    Rm = np.zeros((128, 128), dtype=np.float32)
    for dd in range(64):
        Rm[dd + 64, dd] = -1.0
        Rm[dd, dd + 64] = 1.0
    ident = np.eye(128, dtype=np.float32)
    expc = np.zeros((NE, KR), dtype=np.float32)
    for k in range(NE):
        expc[k, k * RANK:(k + 1) * RANK] = 1.0
    tri2 = np.zeros((128, 1920), dtype=np.float32)
    ww = np.arange(1920)
    for k in range(128):
        tri2[k, ww >= k + 896] = 1.0

    def rows_for_head(qh):
        r0 = 1024 * (qh // 8) + 128 * (qh % 8)
        return w_in[r0:r0 + 128]

    in_maps = []
    for c in range(NCORE):
        winc = np.concatenate(
            [rows_for_head(4 * c + h) for h in range(NH)]
            + [w_in[D + 128 * c: D + 128 * (c + 1)],
               w_in[D + 1024 + 128 * c: D + 1024 + 128 * (c + 1)]], axis=0)
        in_maps.append({
            "xT": np.ascontiguousarray(x2d[TOK * c:TOK * (c + 1), :].T),
            "winT": np.ascontiguousarray(winc.T),
            "woutT": woutT,
            "AT": AT,
            "Bf": Bf,
            "WrT": WrT,
            "cosT": cosT,
            "sinT": sinT,
            "Rm": Rm,
            "ident": ident,
            "expc": expc,
            "tri2": tri2,
        })

    nc = build_program()
    res = run_bass_kernel_spmd(nc, in_maps, core_ids=list(range(NCORE)))
    out = np.concatenate([np.ascontiguousarray(r["outT"].T) for r in res.results],
                         axis=0)
    return out.reshape(1, S, D)


# version 10
# speedup vs baseline: 1.0089x; 1.0089x over previous
import numpy as np
import concourse.bass as bass
import concourse.mybir as mybir
from concourse.bass_utils import run_bass_kernel_spmd

F32 = mybir.dt.float32
BF16 = mybir.dt.bfloat16
AF = mybir.ActivationFunctionType
OP = mybir.AluOpType
AX = mybir.AxisListType.X

NCORE = 8
S = 8192
TOK = 1024
D = 4096
DT = 32
NH = 4
HD = 128
NE = 64
TOPK = 16
RANK = 16
KR = 1024
EPS = 1e-5
EXP_SHIFT = -10.0
SCORE_SCALE = 1.0 / np.sqrt(128.0)
NEG_BIG = -1e30
AGC = 2
AGT = TOK // AGC


class Ctx:
    def __init__(self, nc):
        self.nc = nc
        self.sems = {}
        self.cnt = {}

    def canon(self, name):
        for a, b in (("F2", "FF"), ("F1", "FF"), ("L3", "LL"), ("L2", "LL"),
                     ("L1", "LL")):
            if name.startswith(a):
                name = b + name[len(a):]
                break
        return name

    def sem(self, name):
        name = self.canon(name)
        if name not in self.sems:
            self.sems[name] = self.nc.alloc_semaphore(name)
            self.cnt[name] = 0
        return self.sems[name]

    def inc(self, inst, name, k=1):
        h = self.sem(name)
        name = self.canon(name)
        inst.then_inc(h, k)
        self.cnt[name] += k
        return self.cnt[name]

    def val(self, name):
        self.sem(name)
        return self.cnt[self.canon(name)]

    def wait(self, eng, name, v=None):
        h = self.sem(name)
        if v is None:
            v = self.cnt[self.canon(name)]
        if v > 0:
            eng.wait_ge(h, v)


def build_program(phases=6, p4h=NH, p4q=8, p4sk=(), p1dump=False, p4dump=False):
    nc = bass.Bass("TRN2", target_bir_lowering=False, num_devices=NCORE,
                   detect_race_conditions=False)
    t, v, sc, g, sy = nc.tensor, nc.vector, nc.scalar, nc.gpsimd, nc.sync
    C = Ctx(nc)

    xT_h = nc.dram_tensor("xT", [D, TOK], F32, kind="ExternalInput")
    winT_h = nc.dram_tensor("winT", [D, 768], F32, kind="ExternalInput")
    woutT_h = nc.dram_tensor("woutT", [D, D], F32, kind="ExternalInput")
    AT_h = nc.dram_tensor("AT", [D, KR], F32, kind="ExternalInput")
    Bf_h = nc.dram_tensor("Bf", [KR, D], F32, kind="ExternalInput")
    WrT_h = nc.dram_tensor("WrT", [D, NE], F32, kind="ExternalInput")
    cosT_h = nc.dram_tensor("cosT", [128, S], F32, kind="ExternalInput")
    sinT_h = nc.dram_tensor("sinT", [128, S], F32, kind="ExternalInput")
    Rm_h = nc.dram_tensor("Rm", [128, 128], F32, kind="ExternalInput")
    ident_h = nc.dram_tensor("ident", [128, 128], F32, kind="ExternalInput")
    expc_h = nc.dram_tensor("expc", [NE, KR], F32, kind="ExternalInput")
    tri2_h = nc.dram_tensor("tri2", [128, 1920], F32, kind="ExternalInput")
    outT_h = nc.dram_tensor("outT", [D, TOK], F32, kind="ExternalOutput")

    y2in = [nc.dram_tensor(f"y2in{i}", [D, AGT], BF16) for i in range(AGC)]
    y2out = [nc.dram_tensor(f"y2out{i}", [NCORE * D, AGT], BF16) for i in range(AGC)]
    ctxin = [nc.dram_tensor(f"ctxin{h}", [128, S], BF16) for h in range(NH)]
    ctxout = [nc.dram_tensor(f"ctxout{h}", [NCORE * 128, S], BF16) for h in range(NH)]
    qspill = nc.dram_tensor("qspill", [NH * 128, S], BF16)
    x1spill = nc.dram_tensor("x1spill", [128, DT * TOK], BF16)

    PX = nc.alloc_sbuf_tensor("PX", [128, DT * TOK], BF16)
    PY = nc.alloc_sbuf_tensor("PY", [128, DT * TOK], BF16)
    PU = nc.alloc_sbuf_tensor("PU", [128, 24576], BF16)
    SCR = nc.alloc_sbuf_tensor("SCR", [128, 6144], BF16)
    SCF = nc.alloc_sbuf_tensor("SCF", [128, 1280], F32)
    CONST = nc.alloc_sbuf_tensor("CONST", [128, 2048], BF16)
    onesb = nc.alloc_sbuf_tensor("onesb", [128, 128], BF16)
    onesr = nc.alloc_sbuf_tensor("onesr", [1, 128], F32)
    onescol = nc.alloc_sbuf_tensor("onescol", [128, 1], BF16)
    statr = nc.alloc_sbuf_tensor("statr", [1, 2048], F32)
    cols = nc.alloc_sbuf_tensor("cols", [128, 64], F32)
    zrow = statr[0:1, 0:1024]
    Zf = SCF[:, 256:1280]

    def xt(i, c0=0, w=TOK):
        return PX[:, TOK * i + c0: TOK * i + c0 + w]

    def yt(i, c0=0, w=TOK):
        return PY[:, TOK * i + c0: TOK * i + c0 + w]

    Crope = lambda c0, w: PX[:, 0 * S + c0: 0 * S + c0 + w]
    Srope = lambda c0, w: PX[:, 1 * S + c0: 1 * S + c0 + w]
    KT = lambda c0, w: PX[:, 2 * S + c0: 2 * S + c0 + w]
    Vsb = lambda c0, w: PX[:, 3 * S + c0: 3 * S + c0 + w]

    def y2blk(b, i, c0=0, w=AGT):
        base = b * (DT * AGT)
        return PY[:, base + AGT * i + c0: base + AGT * i + c0 + w]

    def ctxt(e, c0=0, w=TOK):
        return PY[:, TOK * e + c0: TOK * e + c0 + w]

    WrT_sb = lambda i: PU[:, 64 * i: 64 * i + 64]
    ATr = lambda b, i: PU[:, 2048 + b * 4096 + 128 * i: 2048 + b * 4096 + 128 * i + 128]
    Br = lambda b, j: PU[:, 10240 + b * 1024 + 128 * j: 10240 + b * 1024 + 128 * j + 128]
    GE = lambda b, c0, w: PU[:, 12288 + b * 1024 + c0: 12288 + b * 1024 + c0 + w]
    hgT = lambda j, c0, w: PU[:, 14336 + j * 1024 + c0: 14336 + j * 1024 + c0 + w]
    winT_sb = lambda i, f: PU[:, 768 * i + 128 * f: 768 * i + 128 * f + 128]
    qTh = lambda b, c0, w: PU[:, b * S + c0: b * S + c0 + w]
    Pb = lambda b, c0, w: PU[:, 16384 + b * 1024 + c0: 16384 + b * 1024 + c0 + w]
    Zb = lambda c0, w: PU[:, 20480 + c0: 20480 + c0 + w]
    zinv = lambda c0, w: PU[:, 21504 + c0: 21504 + c0 + w]
    ctxst = lambda b, c0, w: PU[:, 22528 + b * 1024 + c0: 22528 + b * 1024 + c0 + w]
    wcol = lambda b, e: PU[:, b * 4096 + 128 * e: b * 4096 + 128 * e + 128]

    mb_sb = lambda c0, w: SCR[:, 2048 + c0: 2048 + c0 + w]
    ib_sb = lambda c0, w: SCR[:, 3072 + c0: 3072 + c0 + w]
    tmp_sb = lambda c0, w: SCR[:, 4096 + c0: 4096 + c0 + w]
    xsq = lambda b, c0, w: SCR[:, 4096 + b * 1024 + c0: 4096 + b * 1024 + c0 + w]
    mask_sb = SCR[:, 0:64]
    G_sb = lambda j: SCR[:, 512 + 64 * j: 512 + 64 * j + 64]
    GT_sb = SCR[0:64, 1024:2048]
    qraw = lambda b: SCR[:, 0 + 512 * b: 512 + 512 * b]
    urop = lambda b: SCR[:, 1024 + 512 * b: 1536 + 512 * b]
    t1rop = lambda b: SCR[:, 2048 + 512 * b: 2560 + 512 * b]
    vrw = SCR[:, 3072:3584]
    qst = lambda b: SCR[:, 4096 + 512 * b: 4608 + 512 * b]
    ln2st = lambda b: SCR[:, 5120 + 512 * b: 5632 + 512 * b]

    s_sb = SCF[:, 0:64]
    swork = SCF[:, 64:128]
    e_sb = SCF[:, 128:192]
    outst = lambda b: SCF[:, 256 + 512 * b: 768 + 512 * b]

    identb = CONST[:, 0:128]
    Rb = CONST[:, 128:256]
    expcb = CONST[0:64, 256:1280]
    tri2b = CONST[:, 0:1920]

    mx = cols[:, 0:8]
    negmx = cols[:, 8:16]
    thr = cols[:, 16:24]
    zc = cols[:, 24:32]
    zic = cols[:, 32:40]
    CST_EXPSHIFT = cols[:, 40:41]
    CST_SCALE = cols[:, 41:42]

    statr_mean = statr[0:1, 0:1024]
    statr_istd = statr[0:1, 1024:2048]
    statr_t = SCF[0:1, 0:1024]
    ones_col = onescol[:, 0:1]

    def vfence(sem_name):
        C.inc(v.tensor_copy(cols[:, 49:50], cols[:, 48:49]), sem_name)

    def sfence(sem_name):
        C.inc(sc.copy(cols[:, 51:52], cols[:, 50:51]), sem_name)

    g.memset(onesb[:, :], 1.0)
    g.memset(onesr[:, :], 1.0)
    g.memset(onescol[:, :], 1.0)
    g.memset(CST_EXPSHIFT, EXP_SHIFT)
    g.memset(CST_SCALE, SCORE_SCALE)
    g.memset(cols[:, 48:52], 0.0)
    C.inc(g.dma_start(out=identb, in_=ident_h[:, :]), "d_cst", 16)
    C.inc(g.dma_start(out=Rb, in_=Rm_h[:, :]), "d_cst", 16)
    C.inc(g.dma_start(out=expcb, in_=expc_h[:, :]), "d_cst", 16)
    for b8 in range(8):
        C.inc(g.dma_start(
            out=PX[:, 4 * TOK * b8: 4 * TOK * (b8 + 1)].rearrange("p (t n) -> p t n", t=4),
            in_=xT_h.ap().rearrange("(t p) n -> p t n", p=128)[:, 4 * b8: 4 * b8 + 4, :],
        ), "d_x", 16)
    C.inc(g.dma_start(out=PU[:, 0:2048].rearrange("p (t n) -> p t n", t=DT),
                      in_=WrT_h.ap().rearrange("(t p) n -> p t n", p=128)), "d_wr", 16)

    def emit_dump_and_stop():
        C.wait(g, "F1vres", C.val("F1vres"))
        C.wait(g, "v_r5", C.val("v_r5"))
        C.wait(g, "L2ln", C.val("L2ln"))
        C.wait(g, "cc_y2", C.val("cc_y2"))
        C.wait(g, "cc_ctx", C.val("cc_ctx"))
        for _hh in range(2):
            for _p in range(2):
                C.wait(g, f"d_ctxh{_hh}p{_p}", C.val(f"d_ctxh{_hh}p{_p}"))
        C.wait(g, "d_qsp0", C.val("d_qsp0"))
        C.wait(g, "d_qsp1", C.val("d_qsp1"))
        C.wait(g, "t_av", C.val("t_av"))
        for b8 in range(8):
            C.inc(g.dma_start(
                out=outT_h.ap().rearrange("(t p) n -> p t n", p=128)[:, 4 * b8:4 * b8 + 4, :],
                in_=PX[:, 4 * TOK * b8:4 * TOK * (b8 + 1)].rearrange("p (t n) -> p t n", t=4)),
                "d_dump", 16)
        C.wait(g, "d_dump", 16 * 8)

    def emit_ln(tag, src_tile, dst_tile, chunked_dma=None):
        v_tag, t_tag, s_tag = f"{tag}v", f"{tag}t", f"{tag}s"
        tb = C.val(t_tag)
        with (
            nc.psum_tensor(f"{tag}psum", [1, 1024], F32) as ps_s,
            nc.psum_tensor(f"{tag}psq", [1, 1024], F32) as ps_q,
        ):
            for i in range(DT):
                if i >= 2:
                    C.wait(v, t_tag, tb + i - 1)
                v.tensor_mul(xsq(i % 2, 0, TOK), src_tile(i), src_tile(i))
                vfence(v_tag)
                for h2 in range(2):
                    t.matmul(ps_s[0:1, 512 * h2:512 * (h2 + 1)], ones_col,
                             src_tile(i, 512 * h2, 512), start=(i == 0), stop=(i == DT - 1))
                C.wait(t, v_tag)
                inst = None
                for h2 in range(2):
                    inst = t.matmul(ps_q[0:1, 512 * h2:512 * (h2 + 1)], ones_col,
                                    xsq(i % 2, 512 * h2, 512), start=(i == 0), stop=(i == DT - 1))
                C.inc(inst, t_tag)
            C.wait(v, t_tag)
            v.tensor_scalar(statr_mean, ps_s[0:1, :], 1.0 / D, None, op0=OP.mult)
            v.tensor_scalar(statr_t, ps_q[0:1, :], 1.0 / D, None, op0=OP.mult)
            v.tensor_mul(statr_istd, statr_mean, statr_mean)
            v.tensor_sub(statr_t, statr_t, statr_istd)
            v.tensor_scalar(statr_t, statr_t, EPS, None, op0=OP.add)
            vfence(v_tag)
            C.wait(sc, v_tag)
            sc.activation(statr_t, statr_t, AF.Sqrt)
            sfence(s_tag)
            C.wait(v, s_tag)
            v.reciprocal(statr_istd, statr_t)
            vfence(v_tag)
        with (
            nc.psum_tensor(f"{tag}pmb", [128, 1024], F32) as ps_mb,
            nc.psum_tensor(f"{tag}pib", [128, 1024], F32) as ps_ib,
        ):
            C.wait(t, v_tag)
            inst = None
            for h2 in range(2):
                t.matmul(ps_mb[:, 512 * h2:512 * (h2 + 1)], onesr[:, :],
                         statr_mean[:, 512 * h2:512 * (h2 + 1)], start=True, stop=True)
                inst = t.matmul(ps_ib[:, 512 * h2:512 * (h2 + 1)], onesr[:, :],
                                statr_istd[:, 512 * h2:512 * (h2 + 1)], start=True, stop=True)
            C.inc(inst, t_tag)
            C.wait(sc, t_tag)
            sc.copy(mb_sb(0, TOK), ps_mb[:, :])
            sc.copy(ib_sb(0, TOK), ps_ib[:, :])
            sfence(s_tag)
            C.wait(v, s_tag)
            if chunked_dma is None:
                for i in range(DT):
                    v.tensor_sub(tmp_sb(0, TOK), src_tile(i), mb_sb(0, TOK))
                    v.tensor_mul(dst_tile(i), tmp_sb(0, TOK), ib_sb(0, TOK))
                    vfence(f"{tag}ln")
            else:
                n = 0
                dma_log = []
                for ct in range(AGC):
                    for i in range(DT):
                        b = n % 2
                        if n >= 2:
                            pname, pcnt = dma_log[n - 2]
                            C.wait(v, pname, pcnt)
                        v.tensor_sub(tmp_sb(0, AGT), src_tile(i, AGT * ct, AGT),
                                     mb_sb(AGT * ct, AGT))
                        v.tensor_mul(ln2st(b), tmp_sb(0, AGT), ib_sb(AGT * ct, AGT))
                        vfence(f"{tag}ln")
                        C.wait(sy, f"{tag}ln")
                        sname = f"{tag}dma{ct}p{b}"
                        cnt = C.inc(chunked_dma(i, ct, ln2st(b)), sname, 16)
                        dma_log.append((sname, cnt))
                        n += 1

    def emit_ffn(tag, ytile, res_out_dma):
        tsc, vsc, vtk, se, vG, tgt, sgt = (f"{tag}tsc", f"{tag}vsc", f"{tag}vtk",
                                           f"{tag}se", f"{tag}vG", f"{tag}tgt",
                                           f"{tag}sgt")
        tge, sge, th, vhg, to, vres = (f"{tag}tge", f"{tag}sge", f"{tag}th",
                                       f"{tag}vhg", f"{tag}to", f"{tag}vres")
        vs_ = f"{tag}vs"
        dA, dB = f"{tag}dA", f"{tag}dB"
        se_b, vG_b, tgt_b, sgt_b = C.val(se), C.val(vG), C.val(tgt), C.val(sgt)
        vsc_b, tge_b, sge_b, th_b = C.val(vsc), C.val(tge), C.val(sge), C.val(th)
        vhg_b, to_b, dA_b, dB_b = C.val(vhg), C.val(to), C.val(dA), C.val(dB)
        with (
            nc.psum_tensor(f"{tag}psc", [128, 1024], F32) as ps_sc,
            nc.psum_tensor(f"{tag}pgt", [64, 128], BF16) as ps_gt,
        ):
            C.wait(t, "d_wr")
            for j in range(8):
                if j >= 2:
                    C.wait(t, vsc, vsc_b + j - 1)
                inst = None
                for i in range(DT):
                    inst = t.matmul(ps_sc[:, 512 * (j % 2):512 * (j % 2) + 64],
                                    ytile(i, 128 * j, 128), WrT_sb(i),
                                    start=(i == 0), stop=(i == DT - 1))
                C.inc(inst, tsc)
                C.wait(v, tsc)
                if j >= 1:
                    C.wait(v, se, se_b + 2 * j)
                C.inc(v.tensor_copy(s_sb, ps_sc[:, 512 * (j % 2):512 * (j % 2) + 64]),
                      vsc)
                v.tensor_copy(swork, s_sb)
                for r_ in range(TOPK):
                    dstm = mx[:, j:j + 1] if r_ == 0 else thr[:, j:j + 1]
                    C.inc(v.reduce_max(dstm, swork, axis=AX), vs_)
                    C.wait(v, vs_)
                    v.tensor_scalar(e_sb, swork, dstm, NEG_BIG,
                                    op0=OP.is_equal, op1=OP.mult)
                    v.tensor_add(swork, swork, e_sb)
                v.tensor_scalar(negmx[:, j:j + 1], mx[:, j:j + 1], -1.0, None,
                                op0=OP.mult)
                C.inc(v.tensor_scalar(mask_sb, s_sb, thr[:, j:j + 1], None,
                                      op0=OP.is_ge), vtk)
                C.wait(sc, vtk)
                C.inc(sc.activation(e_sb, s_sb, AF.Exp, bias=negmx[:, j:j + 1]), se)
                sfence(se)
                C.wait(v, se)
                v.tensor_mul(e_sb, e_sb, mask_sb)
                C.inc(v.reduce_sum(zc[:, j:j + 1], e_sb, axis=AX), vs_)
                C.wait(v, vs_)
                C.inc(v.reciprocal(zic[:, j:j + 1], zc[:, j:j + 1]), vs_)
                C.wait(v, vs_)
                v.tensor_scalar(G_sb(j), e_sb, zic[:, j:j + 1], None,
                                op0=OP.mult)
                vfence(vG)
                C.wait(t, vG, vG_b + j + 1)
                if j >= 1:
                    C.wait(t, sgt, sgt_b + j)
                C.inc(t.transpose(ps_gt[:, :], G_sb(j), identb), tgt)
                C.wait(sc, tgt, tgt_b + j + 1)
                C.inc(sc.copy(GT_sb[:, 128 * j:128 * j + 128], ps_gt[:, :]), sgt)
                if j == 7:
                    sfence(sgt)
        for q_ in range(4):
            C.inc(g.dma_start(
                out=PU[:, 2048 + 1024 * q_:2048 + 1024 * (q_ + 1)]
                    .rearrange("p (t n) -> p t n", t=8),
                in_=AT_h.ap()[:, 0:128]
                    .rearrange("(t p) n -> p t n", p=128)[:, 8 * q_:8 * q_ + 8, :]),
                f"{dA}p0", 16)
        with (
            nc.psum_tensor(f"{tag}ph", [128, 1024], F32) as ps_h,
            nc.psum_tensor(f"{tag}pge", [128, 1024], F32) as ps_ge,
        ):
            C.wait(t, sgt, sgt_b + 9)
            for j in range(8):
                if j + 1 < 8:
                    if j + 1 >= 2:
                        C.wait(g, th, th_b + 2 * j)
                    for q_ in range(4):
                        C.inc(g.dma_start(
                            out=PU[:, 2048 + ((j + 1) % 2) * 4096 + 1024 * q_:
                                   2048 + ((j + 1) % 2) * 4096 + 1024 * (q_ + 1)]
                                .rearrange("p (t n) -> p t n", t=8),
                            in_=AT_h.ap()[:, 128 * (j + 1):128 * (j + 2)]
                                .rearrange("(t p) n -> p t n", p=128)[:, 8 * q_:8 * q_ + 8, :]),
                            f"{dA}p{(j + 1) % 2}", 16)
                if j >= 1:
                    C.wait(t, sge, sge_b + 2 * j)
                if j >= 2:
                    C.wait(t, vhg, vhg_b + 2 * (j - 1))
                inst = None
                for h2 in range(2):
                    inst = t.matmul(ps_ge[:, 512 * h2:512 * (h2 + 1)],
                                    expcb[:, 128 * j:128 * j + 128],
                                    GT_sb[:, 512 * h2:512 * (h2 + 1)],
                                    start=True, stop=True)
                C.inc(inst, tge)
                C.wait(sc, tge, tge_b + j + 1)
                C.inc(sc.copy(GE(j % 2, 0, TOK), ps_ge[:, :]), sge)
                sfence(sge)
                C.wait(t, f"{dA}p{j % 2}", C.val(f"{dA}p{j % 2}"))
                if j >= 1:
                    C.wait(t, vhg, vhg_b + 2 * j)
                for t2 in range(2):
                    inst = None
                    for i in range(DT):
                        inst = t.matmul(ps_h[:, 512 * t2:512 * (t2 + 1)],
                                        ATr(j % 2, i), ytile(i, 512 * t2, 512),
                                        start=(i == 0), stop=(i == DT - 1))
                    C.inc(inst, th)
                    C.wait(v, th)
                    C.wait(v, sge, sge_b + 2 * (j + 1))
                    C.inc(v.tensor_mul(hgT(j, 512 * t2, 512),
                                       ps_h[:, 512 * t2:512 * (t2 + 1)],
                                       GE(j % 2, 512 * t2, 512)), vhg)
                    if j == 7 and t2 == 1:
                        vfence(vhg)
        C.inc(g.dma_start(
            out=PU[:, 10240:10240 + 1024].rearrange("p (t n) -> p t n", t=8),
            in_=Bf_h.ap()[:, 0:128].rearrange("(t p) n -> p t n", p=128)), f"{dB}p0", 16)
        with nc.psum_tensor(f"{tag}po", [128, 1024], F32) as ps_o:
            C.wait(t, vhg, vhg_b + 17)
            for i in range(DT):
                if i + 1 < DT:
                    if i + 1 >= 2:
                        C.wait(g, to, to_b + 2 * i)
                    C.inc(g.dma_start(
                        out=PU[:, 10240 + ((i + 1) % 2) * 1024:
                               10240 + ((i + 1) % 2) * 1024 + 1024]
                            .rearrange("p (t n) -> p t n", t=8),
                        in_=Bf_h.ap()[:, 128 * (i + 1):128 * (i + 2)]
                            .rearrange("(t p) n -> p t n", p=128)), f"{dB}p{(i + 1) % 2}", 16)
                C.wait(t, f"{dB}p{i % 2}", C.val(f"{dB}p{i % 2}"))
                if i >= 1:
                    C.wait(t, vres, C.val(vres))
                for t2 in range(2):
                    inst = None
                    for j in range(8):
                        inst = t.matmul(ps_o[:, 512 * t2:512 * (t2 + 1)],
                                        Br(i % 2, j), hgT(j, 512 * t2, 512),
                                        start=(j == 0), stop=(j == 7))
                    C.inc(inst, to)
                    C.wait(v, to)
                    if res_out_dma is None:
                        v.tensor_add(xt(i, 512 * t2, 512), xt(i, 512 * t2, 512),
                                     ps_o[:, 512 * t2:512 * (t2 + 1)])
                        vfence(vres)
                    else:
                        n = 2 * i + t2
                        if n >= 2:
                            C.wait(v, f"{tag}doutp{n % 2}", C.val(f"{tag}doutp{n % 2}"))
                        v.tensor_add(outst(n % 2),
                                     ps_o[:, 512 * t2:512 * (t2 + 1)],
                                     xt(i, 512 * t2, 512))
                        vfence(vres)
                        C.wait(sy, vres)
                        C.inc(res_out_dma(i, t2, outst(n % 2)), f"{tag}doutp{n % 2}", 16)

    C.wait(v, "d_x", 16 * 8)
    C.wait(t, "d_x", 16 * 8)
    emit_ln("L1", lambda i, c0=0, w=TOK: xt(i, c0, w), lambda i: yt(i))

    C.wait(t, "L1ln", C.val("L1ln"))
    C.wait(t, "d_cst", 48)
    emit_ffn("F1", lambda i, c0=0, w=TOK: yt(i, c0, w), None)

    if phases <= 1:
        if p1dump:
            dbgY = nc.dram_tensor("dbgY", [128, DT * TOK], F32, kind="ExternalOutput")
            dbgH = nc.dram_tensor("dbgH", [128, 8192], F32, kind="ExternalOutput")
            dbgG = nc.dram_tensor("dbgG", [64, 1024], F32, kind="ExternalOutput")
            C.wait(g, "F1vres", C.val("F1vres"))
            for b8 in range(8):
                C.inc(g.dma_start(out=dbgY.ap()[:, 4096 * b8:4096 * (b8 + 1)],
                                  in_=PY[:, 4096 * b8:4096 * (b8 + 1)]), "d_dbg", 16)
            for b2 in range(2):
                C.inc(g.dma_start(out=dbgH.ap()[:, 4096 * b2:4096 * (b2 + 1)],
                                  in_=PU[:, 14336 + 4096 * b2:14336 + 4096 * (b2 + 1)]),
                      "d_dbg", 16)
            C.inc(g.dma_start(out=dbgG.ap()[:, :], in_=GT_sb), "d_dbg", 16)
            for b8 in range(8):
                C.inc(g.dma_start(out=dbgY2.ap()[4096 * b8:4096 * (b8 + 1), :],
                                  in_=y2out[1].ap()[4096 * b8:4096 * (b8 + 1), :]),
                      "d_dbg", 16)
            C.inc(g.dma_start(out=dbgY2i.ap()[:, :], in_=y2in[1].ap()[:, :]),
                  "d_dbg", 16)
            C.wait(g, "d_dbg", C.val("d_dbg"))
        emit_dump_and_stop()
        return nc
    C.wait(g, "F1to", C.val("F1to"))
    C.wait(g, "F1vhg", C.val("F1vhg"))
    for b8 in range(8):
        C.inc(g.dma_start(
            out=PU[:, 3072 * b8: 3072 * (b8 + 1)].rearrange("p (t n) -> p t n", t=4),
            in_=winT_h.ap().rearrange("(t p) n -> p t n", p=128)[:, 4 * b8:4 * b8 + 4, :]),
            "d_win", 16)

    def ln2_dma(i, ct, stage):
        return sy.dma_start(out=y2in[ct].ap()[128 * i:128 * (i + 1), :], in_=stage)

    C.wait(v, "F1vres", C.val("F1vres"))
    C.wait(t, "F1vres", C.val("F1vres"))
    emit_ln("L2", lambda i, c0=0, w=TOK: xt(i, c0, w), None, chunked_dma=ln2_dma)

    for ct in range(AGC):
        C.wait(g, f"L2dma{ct}p0", C.val(f"L2dma{ct}p0"))
        C.wait(g, f"L2dma{ct}p1", C.val(f"L2dma{ct}p1"))
        C.inc(g.collective_compute(
            "AllGather", OP.bypass, replica_groups=[list(range(NCORE))],
            ins=[y2in[ct].ap().opt()], outs=[y2out[ct].ap().opt()]), "cc_y2")

    C.wait(g, "L2ln", C.val("L2ln"))
    for b8 in range(8):
        C.inc(g.dma_start(out=x1spill.ap()[:, 4096 * b8:4096 * (b8 + 1)],
                          in_=PX[:, 4096 * b8:4096 * (b8 + 1)]), "d_x1s", 16)
    C.wait(g, "d_x1s", 16 * 8)
    for b4 in range(4):
        C.inc(g.dma_start(out=PX[:, 0 * S + 2048 * b4: 0 * S + 2048 * (b4 + 1)],
                          in_=cosT_h.ap()[:, 2048 * b4:2048 * (b4 + 1)]), "d_cs", 16)
        C.inc(g.dma_start(out=PX[:, 1 * S + 2048 * b4: 1 * S + 2048 * (b4 + 1)],
                          in_=sinT_h.ap()[:, 2048 * b4:2048 * (b4 + 1)]), "d_cs", 16)

    if phases <= 2:
        emit_dump_and_stop()
        return nc
    with (
        nc.psum_tensor("p3a", [128, 512], F32) as p3a,
        nc.psum_tensor("p3b", [128, 512], F32) as p3b,
        nc.psum_tensor("p3r", [128, 512], F32) as p3r,
        nc.psum_tensor("p3t0", [128, 128], BF16) as p3t0,
        nc.psum_tensor("p3t1", [128, 128], BF16) as p3t1,
    ):
        p3ab = [p3a, p3b]
        p3ts = [p3t0, p3t1]
        C.wait(sc, "L2ln", C.val("L2ln"))
        C.wait(t, "d_win", 16 * 8)
        C.wait(v, "d_cs", 16 * 8)
        nblk = 0
        for ct in range(AGC):
            C.wait(sy, "cc_y2", ct + 1)
            for r in range(NCORE):
                b = nblk % 2
                if nblk >= 2:
                    C.wait(sy, "t_qkv", 6 * (nblk - 1))
                for q_ in range(4):
                    C.inc(sy.dma_start(
                        out=PY[:, b * (DT * AGT) + 8 * AGT * q_:
                               b * (DT * AGT) + 8 * AGT * (q_ + 1)]
                            .rearrange("p (t n) -> p t n", t=8),
                        in_=y2out[ct].ap()[D * r:D * (r + 1), :]
                            .rearrange("(t p) n -> p t n", p=128)[:, 8 * q_:8 * q_ + 8, :]),
                        f"d_y2bp{b}", 16)
                C.wait(t, f"d_y2bp{b}", C.val(f"d_y2bp{b}"))
                tok0 = r * TOK + ct * AGT
                for f in range(6):
                    pf = p3ab[f % 2]
                    C.wait(t, "s_qe", max(0, C.val("s_qe") - 1))
                    inst = None
                    for i in range(DT):
                        inst = t.matmul(pf[:, :], winT_sb(i, f), y2blk(b, i),
                                        start=(i == 0), stop=(i == DT - 1))
                    C.inc(inst, "t_qkv")
                    C.wait(sc, "t_qkv")
                    if f < 5:
                        C.wait(sc, "v_qf", C.val("v_qf"))
                        C.inc(sc.copy(qraw(f % 2), pf[:, :]), "s_qe")
                        sfence("s_qf")
                        C.wait(v, "s_qf", C.val("s_qf"))
                        C.wait(v, "t_rot", C.val("t_rot"))
                        v.tensor_mul(urop(f % 2), qraw(f % 2), Srope(tok0, AGT))
                        vfence("v_u")
                        C.wait(t, "v_u")
                        C.wait(t, "v_qf", C.val("v_qf"))
                        C.inc(t.matmul(p3r[:, :], Rb, urop(f % 2),
                                       start=True, stop=True), "t_rot")
                        v.tensor_mul(t1rop(f % 2), qraw(f % 2), Crope(tok0, AGT))
                        C.wait(v, "t_rot")
                        if f < 4:
                            C.wait(v, f"d_qsp{f % 2}", C.val(f"d_qsp{f % 2}"))
                            dst = qst(f % 2)
                        else:
                            dst = KT(tok0, AGT)
                        v.tensor_add(dst, t1rop(f % 2), p3r[:, :])
                        vfence("v_qf")
                        if f < 4:
                            C.wait(sy, "v_qf")
                            C.inc(sy.dma_start(
                                out=qspill.ap()[128 * f:128 * (f + 1), tok0:tok0 + AGT],
                                in_=qst(f % 2)), f"d_qsp{f % 2}", 16)
                    else:
                        C.wait(sc, "t_vt", C.val("t_vt"))
                        C.inc(sc.copy(vrw, pf[:, :]), "s_qe")
                        sfence("s_qf")
                        C.wait(t, "s_qf", C.val("s_qf"))
                        for q4 in range(4):
                            if C.val("t_vt") >= 2:
                                C.wait(t, "s_vt", C.val("t_vt") - 1)
                            C.inc(t.transpose(p3ts[q4 % 2][:, :],
                                              vrw[:, 128 * q4:128 * q4 + 128],
                                              identb), "t_vt")
                            C.wait(sc, "t_vt")
                            C.inc(sc.copy(Vsb(tok0 + 128 * q4, 128),
                                          p3ts[q4 % 2][:, :]), "s_vt")
                nblk += 1
        sfence("s_vt")
        C.wait(g, "t_rot", C.val("t_rot"))
        C.wait(g, "t_vt", C.val("t_vt"))
    C.inc(g.dma_start(out=tri2b, in_=tri2_h[:, :]), "d_tri", 16)

    if phases <= 3:
        emit_dump_and_stop()
        return nc
    with (
        nc.psum_tensor("p4s0", [128, 1024], F32) as p4s0,
        nc.psum_tensor("p4s1", [128, 1024], F32) as p4s1,
        nc.psum_tensor("p4c", [128, 1024], F32) as p4c,
        nc.psum_tensor("p4z", [1, 1024], F32) as p4z,
    ):
        p4s = [p4s0, p4s1]
        C.wait(v, "d_tri", 16)
        C.wait(t, "v_qf", C.val("v_qf"))
        C.wait(t, "s_vt", C.val("s_vt"))
        C.wait(sy, "t_qkv", C.val("t_qkv"))
        for hh in range(2):
            for b4 in range(4):
                C.inc(sy.dma_start(out=qTh(hh % 2, 2048 * b4, 2048),
                                   in_=qspill.ap()[128 * hh:128 * (hh + 1),
                                                   2048 * b4:2048 * (b4 + 1)]),
                      f"d_qhp{hh % 2}", 16)
        ctx_dma_log = []
        for hh in range(p4h):
            bq = hh % 2
            C.wait(t, f"d_qhp{hh % 2}", 64 * (hh // 2 + 1))
            for qc in range(p4q):
                KB = 8 * (qc + 1)
                se_b = C.val("s_exp")
                vz_b = C.val("v_z4")
                av_b = C.val("t_av")
                zf_fresh = True
                zb_fresh = True

                def emit_vec(kbv):
                    nonlocal zf_fresh, zb_fresh
                    C.wait(v, "s_exp", se_b + kbv + 2)
                    if kbv == 0:
                        C.wait(v, "t_z", C.val("t_z"))
                    if kbv >= 8 * qc:
                        qlo = 128 * (kbv - 8 * qc)
                        v.tensor_mul(Pb(kbv % 4, 0, TOK), Pb(kbv % 4, 0, TOK),
                                     tri2b[:, 896 - qlo: 1920 - qlo])
                    if zb_fresh:
                        inst = v.tensor_copy(Zb(0, TOK), Pb(kbv % 4, 0, TOK))
                        zb_fresh = False
                    else:
                        inst = v.tensor_add(Zb(0, TOK), Zb(0, TOK), Pb(kbv % 4, 0, TOK))
                    if kbv % 8 == 7 or kbv == KB - 1:
                        if zf_fresh:
                            inst = v.tensor_copy(Zf[:, :], Zb(0, TOK))
                            zf_fresh = False
                        else:
                            inst = v.tensor_add(Zf[:, :], Zf[:, :], Zb(0, TOK))
                        zb_fresh = True
                    C.inc(inst, "v_z4")

                def emit_av(kbv, last):
                    C.wait(t, "v_z4", vz_b + kbv + 1)
                    inst = None
                    for h2 in range(2):
                        inst = t.matmul(p4c[:, 512 * h2:512 * (h2 + 1)],
                                        Vsb(128 * kbv, 128),
                                        Pb(kbv % 4, 512 * h2, 512),
                                        start=(kbv == 0), stop=last)
                    C.inc(inst, "t_av")

                def emit_score_exp(kb):
                    pb = kb % 2
                    pp = kb % 4
                    if kb == 0:
                        C.wait(t, "v_zi", C.val("v_zi"))
                        C.wait(t, "v_ctx", C.val("v_ctx"))
                    if kb >= 1:
                        C.wait(t, "s_exp", se_b + max(0, kb - 1))
                    inst = None
                    for h2 in range(2):
                        inst = t.matmul(p4s[pb][:, 512 * h2:512 * (h2 + 1)],
                                        KT(128 * kb, 128),
                                        qTh(bq, 1024 * qc + 512 * h2, 512),
                                        start=True, stop=True)
                    C.inc(inst, "t_sc4")
                    C.wait(sc, "t_sc4")
                    if kb >= 4:
                        C.wait(sc, "t_av", av_b + kb - 3)
                        C.wait(sc, "v_z4", vz_b + kb - 3)
                    C.inc(sc.activation(Pb(pp, 0, TOK), p4s[pb][:, :], AF.Exp,
                                        bias=CST_EXPSHIFT, scale=CST_SCALE), "s_exp")

                for kb2 in range(0, KB, 2):
                    emit_score_exp(kb2)
                    emit_score_exp(kb2 + 1)
                    if kb2 >= 2:
                        emit_vec(kb2 - 2)
                        emit_av(kb2 - 2, last=False)
                        emit_vec(kb2 - 1)
                        emit_av(kb2 - 1, last=False)
                sfence("s_exp")
                emit_vec(KB - 2)
                emit_av(KB - 2, last=False)
                emit_vec(KB - 1)
                vfence("v_z4")
                emit_av(KB - 1, last=True)
                C.wait(sc, "v_z4", C.val("v_z4"))
                C.wait(sc, "t_z", C.val("t_z"))
                sc.copy(Zb(0, TOK), Zf[:, :])
                sfence("s_z16")
                C.wait(t, "s_z16", C.val("s_z16"))
                C.wait(t, "s_exp", C.val("s_exp"))
                C.wait(t, "s_zr", C.val("s_zr"))
                inst = None
                for h2 in range(2):
                    inst = t.matmul(p4z[0:1, 512 * h2:512 * (h2 + 1)], ones_col,
                                    Zb(512 * h2, 512), start=True, stop=True)
                C.inc(inst, "t_z")
                C.wait(sc, "t_z", C.val("t_z"))
                C.wait(sc, "t_zb", C.val("t_zb"))
                sc.copy(zrow, p4z[0:1, :])
                sfence("s_zr")
                C.wait(v, "s_zr", C.val("s_zr"))
                v.reciprocal(statr[0:1, 1024:2048], zrow)
                vfence("v_zr")
                C.wait(t, "v_zr", C.val("v_zr"))
                C.wait(t, "v_zi", C.val("v_zi"))
                inst = None
                for h2 in range(2):
                    inst = t.matmul(p4s0[:, 512 * h2:512 * (h2 + 1)], onesr[:, :],
                                    statr[0:1, 1024 + 512 * h2:1024 + 512 * (h2 + 1)],
                                    start=True, stop=True)
                C.inc(inst, "t_zb")
                C.wait(sc, "t_zb", C.val("t_zb"))
                sc.copy(zinv(0, TOK), p4s0[:, :])
                sfence("v_zi")
                nchunk = hh * 8 + qc
                cb = nchunk % 2
                if nchunk >= 2:
                    pname, pcnt = ctx_dma_log[nchunk - 2]
                    C.wait(v, pname, pcnt)
                C.wait(v, "t_av", C.val("t_av"))
                C.wait(v, "v_zi", C.val("v_zi"))
                for h2 in range(2):
                    v.tensor_mul(ctxst(cb, 512 * h2, 512),
                                 p4c[:, 512 * h2:512 * (h2 + 1)],
                                 zinv(512 * h2, 512))
                vfence("v_ctx")
                C.wait(sy, "v_ctx")
                sname = f"d_ctxh{hh % 2}p{cb}"
                cnt = C.inc(sy.dma_start(out=ctxin[hh].ap()[:, 1024 * qc:1024 * (qc + 1)],
                                         in_=ctxst(cb, 0, TOK)), sname, 16)
                ctx_dma_log.append((sname, cnt))
            if hh + 2 < NH:
                C.wait(sy, "t_sc4", C.val("t_sc4"))
                for b4 in range(4):
                    C.inc(sy.dma_start(out=qTh((hh + 2) % 2, 2048 * b4, 2048),
                                       in_=qspill.ap()[128 * (hh + 2):128 * (hh + 3),
                                                       2048 * b4:2048 * (b4 + 1)]),
                          f"d_qhp{hh % 2}", 16)
            C.wait(g, f"d_ctxh{hh % 2}p0", C.val(f"d_ctxh{hh % 2}p0"))
            C.wait(g, f"d_ctxh{hh % 2}p1", C.val(f"d_ctxh{hh % 2}p1"))
            C.inc(g.collective_compute(
                "AllGather", OP.bypass, replica_groups=[list(range(NCORE))],
                ins=[ctxin[hh].ap().opt()], outs=[ctxout[hh].ap().opt()]), "cc_ctx")

    if phases <= 4:
        if p4dump:
            dbgC = nc.dram_tensor("dbgC", [128, S], F32, kind="ExternalOutput")
            dbgK = nc.dram_tensor("dbgK", [128, S], F32, kind="ExternalOutput")
            dbgV = nc.dram_tensor("dbgV", [128, S], F32, kind="ExternalOutput")
            dbgQ = nc.dram_tensor("dbgQ", [128, S], F32, kind="ExternalOutput")
            dbgY2 = nc.dram_tensor("dbgY2", [NCORE * D, AGT], F32, kind="ExternalOutput")
            dbgY2i = nc.dram_tensor("dbgY2i", [D, AGT], F32, kind="ExternalOutput")
            C.wait(g, "cc_ctx", 1)
            for b4 in range(4):
                C.inc(g.dma_start(out=dbgC.ap()[:, 2048 * b4:2048 * (b4 + 1)],
                                  in_=ctxout[0].ap()[0:128, 2048 * b4:2048 * (b4 + 1)]),
                      "d_dbg", 16)
                C.inc(g.dma_start(out=dbgK.ap()[:, 2048 * b4:2048 * (b4 + 1)],
                                  in_=KT(2048 * b4, 2048)), "d_dbg", 16)
                C.inc(g.dma_start(out=dbgV.ap()[:, 2048 * b4:2048 * (b4 + 1)],
                                  in_=Vsb(2048 * b4, 2048)), "d_dbg", 16)
                C.inc(g.dma_start(out=dbgQ.ap()[:, 2048 * b4:2048 * (b4 + 1)],
                                  in_=qspill.ap()[0:128, 2048 * b4:2048 * (b4 + 1)]),
                      "d_dbg", 16)
            for b8 in range(8):
                C.inc(g.dma_start(out=dbgY2.ap()[4096 * b8:4096 * (b8 + 1), :],
                                  in_=y2out[1].ap()[4096 * b8:4096 * (b8 + 1), :]),
                      "d_dbg", 16)
            C.inc(g.dma_start(out=dbgY2i.ap()[:, :], in_=y2in[1].ap()[:, :]),
                  "d_dbg", 16)
            C.wait(g, "d_dbg", C.val("d_dbg"))
        emit_dump_and_stop()
        return nc
    C.wait(g, "t_av", C.val("t_av"))
    C.wait(g, "t_sc4", C.val("t_sc4"))
    for b8 in range(8):
        C.inc(g.dma_start(out=PX[:, 4096 * b8:4096 * (b8 + 1)],
                          in_=x1spill.ap()[:, 4096 * b8:4096 * (b8 + 1)]), "d_x1r", 16)
    with g.register("ctxoff") as roff:
        pid = g.partition_id()
        for hh in range(NH):
            C.wait(g, "cc_ctx", hh + 1)
            g.reg_alu(roff, pid, TOK, OP.mult)
            src = bass.AP(ctxout[hh], roff, [[S, 128], [S * 128, NCORE], [1, TOK]])
            dst = bass.AP(PY, TOK * hh, [[DT * TOK, 128], [4 * TOK, NCORE], [1, TOK]])
            C.inc(g.dma_start(out=dst, in_=src), "d_ctxr", 16)
    C.wait(g, "t_qkv", C.val("t_qkv"))
    for q_ in range(4):
        C.inc(g.dma_start(
            out=PU[:, 1024 * q_:1024 * (q_ + 1)].rearrange("p (t n) -> p t n", t=8),
            in_=woutT_h.ap()[:, 0:128]
                .rearrange("(t p) n -> p t n", p=128)[:, 8 * q_:8 * q_ + 8, :]), "d_wop0", 16)
    r5_b = C.val("v_r5")
    with (
        nc.psum_tensor("p5o0", [128, 1024], F32) as ps_o50,
        nc.psum_tensor("p5o1", [128, 1024], F32) as ps_o51,
    ):
        C.wait(t, "d_ctxr", 16 * 4)
        C.wait(t, "d_x1r", 16 * 8)
        for fo in range(DT):
            ps_o5 = ps_o50 if fo % 2 == 0 else ps_o51
            if fo + 1 < DT:
                if fo + 1 >= 2:
                    C.wait(g, "t_o5", 2 * fo)
                for q_ in range(4):
                    C.inc(g.dma_start(
                        out=PU[:, ((fo + 1) % 2) * 4096 + 1024 * q_:
                               ((fo + 1) % 2) * 4096 + 1024 * (q_ + 1)]
                            .rearrange("p (t n) -> p t n", t=8),
                        in_=woutT_h.ap()[:, 128 * (fo + 1):128 * (fo + 2)]
                            .rearrange("(t p) n -> p t n", p=128)[:, 8 * q_:8 * q_ + 8, :]),
                        f"d_wop{(fo + 1) % 2}", 16)
            C.wait(t, f"d_wop{fo % 2}", C.val(f"d_wop{fo % 2}"))
            if fo >= 2:
                C.wait(t, "v_r5", r5_b + 2 * (fo - 1))
            for t2 in range(2):
                inst = None
                for e in range(DT):
                    inst = t.matmul(ps_o5[:, 512 * t2:512 * (t2 + 1)],
                                    wcol(fo % 2, e), ctxt(e, 512 * t2, 512),
                                    start=(e == 0), stop=(e == DT - 1))
                C.inc(inst, "t_o5")
                C.wait(v, "t_o5")
                v.tensor_add(xt(fo, 512 * t2, 512), xt(fo, 512 * t2, 512),
                             ps_o5[:, 512 * t2:512 * (t2 + 1)])
                vfence("v_r5")
    C.wait(g, "t_o5", C.val("t_o5"))
    C.inc(g.dma_start(out=identb, in_=ident_h[:, :]), "d_cst", 16)
    C.inc(g.dma_start(out=expcb, in_=expc_h[:, :]), "d_cst", 16)
    C.inc(g.dma_start(out=PU[:, 0:2048].rearrange("p (t n) -> p t n", t=DT),
                      in_=WrT_h.ap().rearrange("(t p) n -> p t n", p=128)), "d_wr", 16)
    C.wait(v, "d_cst", C.val("d_cst"))
    C.wait(t, "d_cst", C.val("d_cst"))
    C.wait(t, "v_r5", C.val("v_r5"))
    C.wait(v, "t_o5", C.val("t_o5"))
    emit_ln("L3", lambda i, c0=0, w=TOK: xt(i, c0, w), lambda i: yt(i))

    if phases <= 5:
        emit_dump_and_stop()
        return nc
    def out_dma(i, t2, stage):
        return sy.dma_start(out=outT_h.ap()[128 * i:128 * (i + 1),
                                            512 * t2:512 * (t2 + 1)], in_=stage)

    C.wait(t, "L3ln", C.val("L3ln"))
    emit_ffn("F2", lambda i, c0=0, w=TOK: yt(i, c0, w), out_dma)

    C.wait(sy, "F2doutp0", C.val("F2doutp0"))
    C.wait(sy, "F2doutp1", C.val("F2doutp1"))
    return nc


def kernel(L0_attn_x, L0_attn_w_in, L0_attn_w_out, L0_ffn_A, L0_ffn_B, L0_ffn_Wr):
    x = np.asarray(L0_attn_x)
    w_in = np.asarray(L0_attn_w_in)
    w_out = np.asarray(L0_attn_w_out)
    A = np.asarray(L0_ffn_A)
    B = np.asarray(L0_ffn_B)
    Wr = np.asarray(L0_ffn_Wr)

    x2d = x.reshape(S, D)
    woutT = np.ascontiguousarray(w_out.T)
    AT = np.ascontiguousarray(A.reshape(KR, D).T)
    Bf = np.ascontiguousarray(B.transpose(0, 2, 1).reshape(KR, D))
    WrT = np.ascontiguousarray(Wr.T)

    inv_freq = (1.0 / (10000.0 ** (np.arange(0, HD, 2, dtype=np.float32)
                                   / np.float32(HD)))).astype(np.float32)
    ang = np.arange(S, dtype=np.float32)[:, None] * inv_freq[None, :]
    cosv = np.cos(ang).astype(np.float32)
    sinv = np.sin(ang).astype(np.float32)
    cosT = np.ascontiguousarray(np.concatenate([cosv.T, cosv.T], axis=0))
    sinT = np.ascontiguousarray(np.concatenate([sinv.T, sinv.T], axis=0))

    Rm = np.zeros((128, 128), dtype=np.float32)
    for dd in range(64):
        Rm[dd + 64, dd] = -1.0
        Rm[dd, dd + 64] = 1.0
    ident = np.eye(128, dtype=np.float32)
    expc = np.zeros((NE, KR), dtype=np.float32)
    for k in range(NE):
        expc[k, k * RANK:(k + 1) * RANK] = 1.0
    tri2 = np.zeros((128, 1920), dtype=np.float32)
    ww = np.arange(1920)
    for k in range(128):
        tri2[k, ww >= k + 896] = 1.0

    def rows_for_head(qh):
        r0 = 1024 * (qh // 8) + 128 * (qh % 8)
        return w_in[r0:r0 + 128]

    in_maps = []
    for c in range(NCORE):
        winc = np.concatenate(
            [rows_for_head(4 * c + h) for h in range(NH)]
            + [w_in[D + 128 * c: D + 128 * (c + 1)],
               w_in[D + 1024 + 128 * c: D + 1024 + 128 * (c + 1)]], axis=0)
        in_maps.append({
            "xT": np.ascontiguousarray(x2d[TOK * c:TOK * (c + 1), :].T),
            "winT": np.ascontiguousarray(winc.T),
            "woutT": woutT,
            "AT": AT,
            "Bf": Bf,
            "WrT": WrT,
            "cosT": cosT,
            "sinT": sinT,
            "Rm": Rm,
            "ident": ident,
            "expc": expc,
            "tri2": tri2,
        })

    nc = build_program()
    res = run_bass_kernel_spmd(nc, in_maps, core_ids=list(range(NCORE)))
    out = np.concatenate([np.ascontiguousarray(r["outT"].T) for r in res.results],
                         axis=0)
    return out.reshape(1, S, D)
